# version 22
# baseline (speedup 1.0000x reference)
import numpy as np
import ml_dtypes
import contextlib
import concourse.bass as bass
import concourse.mybir as mybir
from concourse.bass_utils import run_bass_kernel_spmd

F32 = mybir.dt.float32
BF16 = mybir.dt.bfloat16
I32 = mybir.dt.int32
AF = mybir.ActivationFunctionType
ALU = mybir.AluOpType
AX = mybir.AxisListType

T, D = 4096, 1024
H = 4
SC, NSC, NQ = 512, 8, 4
P = 128
EPS = 1e-5
GAMMA = [1.0 - 2.0 ** (-5 - h) for h in range(H)]


def _log_decay_f32():
    import jax
    import jax.numpy as jnp
    with jax.default_device(jax.devices("cpu")[0]):
        return np.asarray(jnp.log1p(-jnp.exp2(-5.0 - jnp.arange(H, dtype=jnp.float32))))


_LD = None


def _ld():
    global _LD
    if _LD is None:
        _LD = _log_decay_f32().astype(np.float64)
    return _LD


def _gch():
    return [float(np.float32(np.exp(_ld()[h] * SC))) for h in range(H)]
NWG = 5
ATTN_MODE = "quad"
ENGINES = ("pe", "dve", "act", "pool", "syncq")
_ENG_NC = {"syncq": "sync", "act": "scalar", "pool": "gpsimd", "dve": "vector", "pe": "tensor"}


class _Op:
    __slots__ = ("eng", "fn", "deps", "kept", "seq", "inc", "sem", "semval",
                 "clock_before", "tag", "is_dma")

    def __init__(self, eng, fn, seq, tag, is_dma, sem):
        self.eng, self.fn, self.seq, self.tag = eng, fn, seq, tag
        self.is_dma, self.sem = is_dma, sem
        self.deps, self.kept = [], []
        self.inc = False
        self.semval = None
        self.clock_before = None


class PB:

    def __init__(self):
        self.ops = {e: [] for e in ENGINES}
        self.order = []
        self.last_w = {}
        self.readers = {}
        self.dma_count = {}

    def op(self, eng, fn, reads=(), writes=(), tag="", dma_sem=None):
        o = _Op(eng, fn, len(self.ops[eng]), tag, dma_sem is not None, dma_sem)
        if dma_sem is not None:
            self.dma_count[dma_sem] = self.dma_count.get(dma_sem, 0) + 1
            o.semval = 16 * self.dma_count[dma_sem]
        deps = []
        for r in reads:
            w = self.last_w.get(r)
            if w is not None:
                deps.append(w)
        for r in writes:
            w = self.last_w.get(r)
            if w is not None:
                deps.append(w)
            deps.extend(self.readers.get(r, ()))
        seen = set()
        for d in deps:
            if d is o or id(d) in seen:
                continue
            seen.add(id(d))
            if d.eng == "pe" and eng == "pe" and not d.is_dma and not o.is_dma:
                continue
            o.deps.append(d)
        for r in reads:
            self.readers.setdefault(r, []).append(o)
        for r in writes:
            self.last_w[r] = o
            self.readers[r] = []
        self.ops[eng].append(o)
        self.order.append(o)
        return o

    def dma(self, eng, sem, fn, reads=(), writes=(), tag=""):
        return self.op(eng, fn, reads, writes, tag, dma_sem=sem)

    @staticmethod
    def _merge(a, b):
        for k, v in b.items():
            if a.get(k, -1) < v:
                a[k] = v

    def _cclock(self, d):
        c = dict(d.clock_before)
        c[d.sem if d.is_dma else d.eng] = d.semval if d.is_dma else d.seq
        return c

    def finalize(self):
        for o in self.order:
            prev = self.ops[o.eng][o.seq - 1] if o.seq > 0 else None
            clock = {} if prev is None else dict(prev.clock_before)
            if prev is not None:
                for d in prev.kept:
                    self._merge(clock, self._cclock(d))
            o.clock_before = clock
            for d in o.deps:
                dom = d.sem if d.is_dma else d.eng
                val = d.semval if d.is_dma else d.seq
                if clock.get(dom, -1) >= val:
                    continue
                o.kept.append(d)
                self._merge(clock, self._cclock(d))
                if not d.is_dma:
                    d.inc = True
        for e in ENGINES:
            c = 0
            for o in self.ops[e]:
                if o.is_dma:
                    o.inc = True
                elif o.inc:
                    c += 1
                    o.semval = c

    def emit(self, eng_obj, eng_name, sems):
        own = sems[eng_name]
        for o in self.ops[eng_name]:
            waits = {}
            for d in o.kept:
                key = d.sem if d.is_dma else d.eng
                val = d.semval if d.is_dma else (d.semval if d.inc else None)
                if val is None:
                    raise RuntimeError(f"{o.tag}: dep {d.tag} has no semval")
                if waits.get(key, -1) < val:
                    waits[key] = val
            for key in sorted(waits):
                eng_obj.wait_ge(sems[key], waits[key])
            instr = o.fn()
            if o.is_dma:
                instr.then_inc(sems[o.sem], 16)
            elif o.inc:
                instr.then_inc(own, 1)


_TBL_CACHE = {}


def _host_tables(mode=None):
    mode = ATTN_MODE if mode is None else mode
    if _TBL_CACHE:
        return dict(_TBL_CACHE)
    import jax
    import jax.numpy as jnp
    with jax.default_device(jax.devices("cpu")[0]):
        inv_freq = 1.0 / 10000.0 ** (jnp.arange(0, 256, 2, dtype=jnp.float32) / 256)
        freqs = jnp.arange(T, dtype=jnp.float32)[:, None] * inv_freq[None, :]
        cos = np.asarray(jnp.cos(freqs)).astype(np.float64).T
        sin = np.asarray(jnp.sin(freqs)).astype(np.float64).T
    ld = _ld()
    jj = np.arange(P)[:, None]
    cc_ = np.arange(SC)[None, :]
    tri = ((cc_ >= jj) | (cc_ >= P)).astype(np.float32)
    if mode == "linear":
        i_in = (np.arange(T) % SC).astype(np.float64)
        rope = np.zeros((H, 4, 128, T), dtype=np.float64)
        for h in range(H):
            gq = np.exp(ld[h] * i_in) / 16.0
            gk = np.exp(-ld[h] * i_in)
            rope[h, 0] = cos * gq[None, :]
            rope[h, 1] = sin * gq[None, :]
            rope[h, 2] = cos * gk[None, :]
            rope[h, 3] = sin * gk[None, :]
        _TBL_CACHE.update(dict(c_rope=np.ascontiguousarray(rope.astype(np.float32))))
    else:
        rope = np.stack([cos / 16.0, sin / 16.0, cos, sin]).astype(np.float32)
        _TBL_CACHE.update(dict(c_rope=np.ascontiguousarray(rope)))
        import jax
        import jax.numpy as jnp
        ldf32 = _log_decay_f32()
        with jax.default_device(jax.devices("cpu")[0]):
            dr = jnp.arange(0, 4096, dtype=jnp.float32)
            m1 = np.stack([np.asarray(jnp.exp(ldf32[h] * dr)) for h in range(H)])
        qmask = np.zeros((H, 32, P, SC), dtype=np.float32)
        dd = (np.arange(SC)[None, :] - np.arange(P)[:, None])
        for h in range(H):
            for k0 in range(32):
                d = (k0 - 3) * P + dd
                valid = d >= 0
                qmask[h, k0][valid] = m1[h][d[valid]]
        _TBL_CACHE.update(dict(c_qmask=np.ascontiguousarray(qmask)))
    _TBL_CACHE.update(dict(
        c_tri=np.ascontiguousarray(tri),
        c_id=np.eye(P, dtype=ml_dtypes.bfloat16),
        c_ones=np.ones((1, P), dtype=np.float32),
    ))
    return dict(_TBL_CACHE)


def build_program(dbg=False, mode=None):
    mode = ATTN_MODE if mode is None else mode
    nc = bass.Bass("TRN2", target_bir_lowering=False)
    dt_in = {
        "L0_attn_x": ([T, D], F32), "L0_attn_Wq": ([D, D], F32),
        "L0_attn_Wk": ([D, D], F32), "L0_attn_Wv": ([2048, D], F32),
        "L0_attn_Wg": ([2048, D], F32), "L0_attn_Wo": ([D, 2048], F32),
        "L0_attn_g_norm_w": ([512], F32), "L0_ffn_packed_weight": ([262144], I32),
        "L0_ffn_weight_scale": ([1], F32), "L0_ffn_bias": ([D], F32),
        "c_rope": ([H, 4, 128, T] if mode == "linear" else [4, 128, T], F32),
        "c_tri": ([P, SC], F32),
        "c_id": ([P, P], BF16), "c_ones": ([1, P], F32),
    }
    if mode == "quad":
        dt_in["c_qmask"] = ([H, 32, P, SC], F32)
    A = {k: nc.dram_tensor(k, sh, dt, kind="ExternalInput").ap() for k, (sh, dt) in dt_in.items()}
    out = nc.dram_tensor("out", [T, D], F32, kind="ExternalOutput").ap()
    DBG = {}
    if dbg:
        for nm, sh, dt in [("d_x1", [SC, D], F32), ("d_x2", [SC, D], F32),
                           ("d_qp", [P, 8, SC], F32), ("d_kp", [P, 8, SC], F32),
                           ("d_ktm", [P, NQ, D], F32), ("d_v", [P, NQ, 2048], F32),
                           ("d_g", [P, NQ, 2048], F32), ("d_og", [P, NQ, 2048], F32),
                           ("d_s", [P, 8, 512], F32), ("d_sct", [P, NQ, SC], F32),
                           ("d_fmT", [P, 8, SC], F32), ("d_tern", [P, 8, D], F32),
                           ("d_q8", [P, 8, SC], F32), ("d_wq", [D, D], F32),
                           ("d_wv", [D, 2048], F32), ("d_rt", [P, 4, SC], F32),
                           ("d_wg0", [P, 4096], F32)]:
            DBG[nm] = nc.dram_tensor(nm, sh, dt, kind="ExternalOutput").ap()
    scr = {}
    for nm, sh in [("wq_bf", [D, D]), ("wk_bf", [D, D]), ("wv_bf", [2048, D]),
                   ("wg_bf", [2048, D]), ("wo_bf", [D, 2048]),
                   ("wq_wt", [D, D]), ("wk_wt", [D, D]), ("wv_wt", [D, 2048]),
                   ("wg_wt", [D, 2048]), ("wo_wt", [2048, D])]:
        scr[nm] = nc.dram_tensor(nm, sh, BF16, kind="Internal").ap()
    if mode == "quad":
        scr["k_hist"] = nc.dram_tensor("k_hist", [NSC, P, 8, 512], BF16, kind="Internal").ap()
        scr["v_hist"] = nc.dram_tensor("v_hist", [NSC, P, NQ, 2048], BF16, kind="Internal").ap()

    ctx = contextlib.ExitStack()
    sb = lambda nm, shape, dt: ctx.enter_context(nc.sbuf_tensor(nm, shape, dt))
    pst = lambda nm, shape: ctx.enter_context(nc.psum_tensor(nm, shape, F32))
    mksem = lambda nm: ctx.enter_context(nc.semaphore(nm))

    x_sb = sb("x_sb", [P, NQ, D], F32)
    fmT = sb("fmT", [P, 8, SC], BF16)
    qp = sb("qp", [P, 8, SC], BF16)
    kp = sb("kp", [P, 8, SC], BF16)
    ktm = sb("ktm", [P, NQ, D], BF16) if mode == "linear" else None
    v_sb = sb("v_sb", [P, NQ, 2048], BF16)
    g_sb = sb("g_sb", [P, NQ, 2048], BF16)
    ogT = sb("ogT", [P, 16, P], BF16)
    sct = sb("sct", [P, NQ, SC], BF16)
    if mode == "linear":
        scr_s = sb("scr_s", [P, 8, 512], BF16)
        u_s = sb("u_s", [P, 8, 512], BF16)
    else:
        scr_s = u_s = None
        msk = sb("msk", [P, 2, 4, SC], F32)
        khb = sb("khb", [P, 2, 2, SC], BF16)
        vhb = sb("vhb", [P, 2, NQ, SC], BF16)
    wg_sb = [sb(f"wgs{i}", [P, 4096], BF16) for i in range(NWG)]
    tern = sb("tern", [P, 8, D], BF16)
    rt_sb = ([sb("rts0", [P, 4, SC], F32), sb("rts1", [P, 4, SC], F32)]
             if mode == "linear" else [sb("rts0", [P, 4, SC], F32)])
    tri_sb = sb("tri_sb", [P, SC], F32) if mode == "linear" else None
    id_sb = sb("id_sb", [P, P], BF16)
    ones_sb = sb("ones_sb", [1, P], F32)
    bias_sb = sb("bias_sb", [P, D], F32)
    gnw_sb = sb("gnw_sb", [P, 512], F32)
    ws_sb = sb("ws_sb", [P, 1], F32)
    unp_i32 = sb("unp_i32", [P, D], I32)
    stats = sb("stats", [P, 112], F32)
    xs = sb("xs", [P, D], F32)
    y1 = sb("y1", [P, D], I32)
    ct = sb("ct", [P, D], F32)
    stage_bf = sb("stage_bf", [P, D], BF16)
    t1 = sb("t1", [P, 512], F32)
    t2 = sb("t2", [P, 512], F32)
    tt_f = sb("tt_f", [P, D], F32)
    eps_sb = sb("eps_sb", [P, 1], F32)

    pp = [pst(f"pp{i}", [P, 1024]) for i in range(4)]
    PPN = ["pp0", "pp1", "pp2", "pp3"]

    sems = {e: mksem("sm_" + e) for e in ENGINES}
    for nm in (["sx%d" % q for q in range(NQ)] + ["so%d" % q for q in range(NQ)]
               + ["swg%d" % i for i in range(NWG)] + ["srt0", "srt1", "scst", "pc0", "pc1",
                  "ps0", "ps1", "ps2", "ps3", "sid", "sones", "srow", "spk0", "spk1", "sdbg",
                  "skh0", "skh1", "svh0", "svh1", "smk0", "smk1", "skst", "svst"]):
        sems[nm] = mksem("sm_" + nm)

    b = PB()
    R = lambda *a: list(a)

    def ld(eng, sem, dst_ap, src_ap, writes, reads=(), tag="ld"):
        nceng = getattr(nc, _ENG_NC[eng])
        b.dma(eng, sem, lambda d=dst_ap, s=src_ap, e=nceng: e.dma_start(out=d, in_=s),
              reads=reads, writes=writes, tag=tag)

    CM = dict(amax=0, clamp=1, inv=2, alpha=3, mb=4, sq=8, rstd=9, mbneg=10,
              rs=11, mean=12, var=13, mx=14, mn=15)

    def col(q, name):
        c = q * 16 + CM[name]
        return stats[:, c:c + 1]

    def acol(h, qb):
        return stats[:, 64 + 4 * h + qb: 65 + 4 * h + qb]

    def arow(h, base):
        return stats[:, base + 4 * h: base + 4 * h + 4]

    def arc(h, qb):
        return stats[:, 96 + 4 * h + qb: 97 + 4 * h + qb]

    if mode == "linear":
        ld("syncq", "scst", tri_sb[:], A["c_tri"], R(("tri",)))
    ld("syncq", "sid", id_sb[:], A["c_id"], R(("id",)))
    ld("syncq", "sones", ones_sb[:], A["c_ones"], R(("ones",)))
    ld("syncq", "srow", bias_sb[:1, 0:D], A["L0_ffn_bias"][None, :], R(("rowb",)))
    for n in range(2):
        b.op("pe", lambda n=n: nc.tensor.matmul(pp[0][:, n * 512:(n + 1) * 512], ones_sb[:],
                                                bias_sb[:1, n * 512:(n + 1) * 512], start=True, stop=True),
             reads=R(("rowb",), ("ones",)), writes=R(("pp0", n)), tag="pbias%d" % n)
    b.op("act", lambda: nc.scalar.activation(bias_sb[:], pp[0][:], AF.Identity, scale=0.5),
         reads=R(("pp0", 0), ("pp0", 1)), writes=R(("bias",), ("rowb",)), tag="pbiasE")
    ld("syncq", "srow", gnw_sb[:1, 0:512], A["L0_attn_g_norm_w"][None, :], R(("rowg",)), reads=R(("rowb",)))
    b.op("pe", lambda: nc.tensor.matmul(pp[0][:, 0:512], ones_sb[:], gnw_sb[:1, 0:512], start=True, stop=True),
         reads=R(("rowg",), ("ones",), ("bias",)), writes=R(("pp0", 0)), tag="pgnw")
    b.op("act", lambda: nc.scalar.activation(gnw_sb[:], pp[0][:, 0:512], AF.Identity),
         reads=R(("pp0", 0),), writes=R(("gnw",), ("rowg",)), tag="pgnwE")
    ld("syncq", "srow", ws_sb[:1, 0:1], A["L0_ffn_weight_scale"][None, :], R(("roww",)), reads=R(("rowg",)))
    b.op("pe", lambda: nc.tensor.matmul(pp[0][:, 0:1], ones_sb[:], ws_sb[:1, 0:1], start=True, stop=True),
         reads=R(("roww",), ("ones",), ("gnw",)), writes=R(("pp0", 0)), tag="pws")
    b.op("act", lambda: nc.scalar.activation(ws_sb[:], pp[0][:, 0:1], AF.Identity),
         reads=R(("pp0", 0),), writes=R(("ws",), ("roww",)), tag="pwsE")
    b.op("pool", lambda: nc.gpsimd.memset(eps_sb[:], EPS), writes=R(("eps",)), tag="zE")
    if dbg:
        b.op("pool", lambda: nc.gpsimd.memset(sct[:], 0.0), writes=R(*[("sct", jb) for jb in range(NQ)]), tag="zsct")
    if mode == "linear":
        b.op("pool", lambda: nc.gpsimd.memset(scr_s[:], 0.0),
             writes=R(*[("scr", h, c) for h in range(H) for c in range(2)]), tag="zS")
        b.op("pool", lambda: nc.gpsimd.memset(u_s[:], 0.0),
             writes=R(*[("u", h, c) for h in range(H) for c in range(2)]), tag="zU")

    pk = A["L0_ffn_packed_weight"].rearrange("(c p n) -> c p n", p=P, n=D)
    ld("pool", "spk0", y1[:], pk[0], R(("pky",)), tag="pkl0")
    first_chunk1 = [True]
    for pc in (0, 1):
        if pc == 1:
            ld("pool", "spk0", y1[:], pk[1], R(("pky",)), tag="pkl1")
        for oc in range(pc, 8, 2):
            sh = 2 * (oc // 2)
            b.op("dve", lambda sh=sh: nc.vector.tensor_scalar(
                unp_i32[:], y1[:], sh, 3, ALU.logical_shift_right, ALU.bitwise_and),
                reads=R(("pky",),), writes=R(("unp",)), tag=f"unp{oc}")
            b.op("dve", lambda: nc.vector.tensor_scalar(stage_bf[:], unp_i32[:], -1.0, None, ALU.add),
                 reads=R(("unp",),), writes=R(("stage",)), tag=f"unpb{oc}")
            for j in range(8):
                bank = j // 4
                b.op("pe", lambda j=j, bank=bank: nc.tensor.matmul(
                    pp[2][:, bank * 512 + (j % 4) * P: bank * 512 + (j % 4 + 1) * P],
                    stage_bf[:, j * P:(j + 1) * P], id_sb[:], start=True, stop=True),
                    reads=R(("stage",), ("id",)), writes=R(("pp2", bank)), tag=f"tT{oc}{j}")
            b.op("act", lambda oc=oc: nc.scalar.activation(
                tern[:, 0:8, oc * P:(oc + 1) * P],
                pp[2][:].rearrange("p (j n) -> p j n", n=P), AF.Identity),
                reads=R(("pp2", 0), ("pp2", 1)), writes=R(("tern", oc)), tag=f"tE{oc}")

    b.op("pool", lambda: nc.gpsimd.memset(t2[:, 0:2], 0.0),
         writes=[("pky",), ("unp",), ("y1",), ("ct",), ("xs",), ("stage",),
                 ("ttf", 0), ("ttf", 1)], tag="prepbarA")

    v_all = v_sb[:].rearrange("p a (s n) -> p (a s) n", n=D)
    v_all2 = v_sb[:]

    def prep_weight(wname, src, bf, wt, rows, cols, pci, psi):
        nch = rows // P
        seg = src.rearrange("(c p) i -> c p i", p=P)
        bfr = bf.rearrange("(c p) i -> c p i", p=P)
        per_cast = 8 if cols == D else 4
        nh = nch // per_cast
        vv = v_all if cols == D else v_all2
        for hh in range(nh):
            b.dma("pool", f"pc{pci}", lambda hh=hh, seg=seg, vv=vv, per_cast=per_cast: nc.gpsimd.dma_start(
                out=vv[:, 0:per_cast, :],
                in_=seg[hh * per_cast:(hh + 1) * per_cast].rearrange("c p i -> p c i")),
                reads=R(), writes=R(("vstage",)), tag=f"cast:{wname}{hh}")
            b.dma("syncq", f"ps{psi}", lambda hh=hh, bfr=bfr, vv=vv, per_cast=per_cast: nc.sync.dma_start(
                out=bfr[hh * per_cast:(hh + 1) * per_cast].rearrange("c p i -> p c i"),
                in_=vv[:, 0:per_cast, :]),
                reads=R(("vstage",)), writes=R((f"d.{wname}.bf", hh)), tag=f"stbf:{wname}{hh}")
        ncc = cols // P
        per_store = 8192 // rows
        g_view = g_sb[:] if rows == 2048 else g_sb[:].rearrange("p a (s n) -> p (a s) n", n=rows)
        wtr = wt.rearrange("(c p) o -> c p o", p=P)
        done, st_i = 0, 0
        while done < ncc:
            cnt = min(per_store, ncc - done)
            for k in range(cnt):
                c = done + k
                b.dma("syncq", f"ps{psi}", lambda c=c, k=k, bf=bf, rows=rows, g_view=g_view: nc.sync.dma_start(
                    out=g_view[:, k, :], in_=bf[0:rows, c * P:(c + 1) * P], transpose=True),
                    reads=R(*[(f"d.{wname}.bf", hh) for hh in range(nh)], ("gstage",)),
                    writes=R(("gstage", k)), tag=f"tld:{wname}{c}")
            b.dma("syncq", f"ps{(psi + 2) % 4}", lambda done=done, cnt=cnt, wtr=wtr, g_view=g_view: nc.sync.dma_start(
                out=wtr[done:done + cnt].rearrange("c p o -> p c o"), in_=g_view[:, 0:cnt, :]),
                reads=R(*[("gstage", k) for k in range(cnt)]),
                writes=R(("gstage",), (f"d.{wname}.wt", st_i)), tag=f"stwt:{wname}{st_i}")
            done += cnt
            st_i += 1
        return st_i

    GR_PLAN = []
    slot = 0
    for wname, wt, nin_tot, ncols in (("wq", scr["wq_wt"], 8, D), ("wk", scr["wk_wt"], 8, D),
                                      ("wv", scr["wv_wt"], 8, 2048), ("wg", scr["wg_wt"], 8, 2048),
                                      ("wo", scr["wo_wt"], 16, D)):
        per = 4096 // ncols
        for g0 in range(0, nin_tot, per):
            GR_PLAN.append((slot % NWG, wname, wt, g0, per, ncols))
            slot += 1

    def wslot(wname, j):
        for (s, wn, wt, g0, per, ncols) in GR_PLAN:
            if wn == wname and g0 <= j < g0 + per:
                return s, g0, ncols
        raise KeyError((wname, j))

    def emit_granule_loads(sc, which):
        for gi, (s, wname, wt, g0, per, ncols) in enumerate(GR_PLAN):
            if wname not in which:
                continue
            wtr = wt.rearrange("(c p) o -> c p o", p=P)
            b.dma("syncq", f"swg{s}", lambda s=s, wtr=wtr, g0=g0, per=per, ncols=ncols: nc.sync.dma_start(
                out=wg_sb[s][:].rearrange("p (c n) -> p c n", n=ncols)[:, 0:per, :],
                in_=wtr[g0:g0 + per].rearrange("c p o -> p c o")),
                reads=R(*[(f"d.{wname}.wt", i) for i in range(NWT[wname])]),
                writes=R(("wg", s)), tag=f"gld{sc}:{wname}{g0}")
            if dbg and sc == 0 and wname == "wq" and g0 == 0:
                dump("d_wg0", wg_sb[0][:], R(("wg", 0)), "wg0")

    def emit_rope_load(sc, h):
        if mode == "quad":
            if h != 0:
                return
            b.dma("syncq", "srt0", lambda sc=sc: nc.sync.dma_start(
                out=rt_sb[0][:],
                in_=A["c_rope"][:, :, sc * SC:(sc + 1) * SC].rearrange("t p n -> p t n")),
                reads=R(), writes=R(("rt", 0)), tag=f"rtld{sc}")
            return
        b.dma("syncq", f"srt{h % 2}", lambda sc=sc, h=h: nc.sync.dma_start(
            out=rt_sb[h % 2][:],
            in_=A["c_rope"][h, :, :, sc * SC:(sc + 1) * SC].rearrange("t p n -> p t n")),
            reads=R(), writes=R(("rt", h % 2)), tag=f"rtld{sc}.{h}")
        if dbg and sc == 0 and h == 0:
            dump("d_rt", rt_sb[0][:], R(("rt", 0)), "rt0")

    def emit_x_load(sc, q):
        b.dma("act", f"sx{q}", lambda sc=sc, q=q: nc.scalar.dma_start(
            out=x_sb[:, q, :], in_=A["L0_attn_x"][sc * SC + q * P: sc * SC + (q + 1) * P, :]),
            reads=R(), writes=R(("x", q)), tag=f"xld{sc}.{q}")

    NWT = {}
    NWT["wq"] = prep_weight("wq", A["L0_attn_Wq"], scr["wq_bf"], scr["wq_wt"], D, D, 0, 0)
    NWT["wk"] = prep_weight("wk", A["L0_attn_Wk"], scr["wk_bf"], scr["wk_wt"], D, D, 1, 1)
    emit_granule_loads(0, ("wq", "wk"))
    emit_rope_load(0, 0)
    NWT["wv"] = prep_weight("wv", A["L0_attn_Wv"], scr["wv_bf"], scr["wv_wt"], 2048, D, 0, 0)
    NWT["wg"] = prep_weight("wg", A["L0_attn_Wg"], scr["wg_bf"], scr["wg_wt"], 2048, D, 1, 1)
    NWT["wo"] = prep_weight("wo", A["L0_attn_Wo"], scr["wo_bf"], scr["wo_wt"], D, 2048, 0, 0)

    def stats_ln(tg, q, want_minmax):
        b.op("act", lambda q=q: nc.scalar.activation(y1[:], x_sb[:, q, :], AF.Identity, accum_out=col(q, "sq")),
             reads=R(("x", q)), writes=R(("y1",), ("stc", q, 1)), tag=f"{tg}.sum{q}")
        b.op("act", lambda q=q: nc.scalar.activation(y1[:], x_sb[:, q, :], AF.Square, accum_out=col(q, "var")),
             reads=R(("x", q)), writes=R(("y1",), ("stc", q, 2)), tag=f"{tg}.ssq{q}")
        if want_minmax:
            b.op("dve", lambda q=q: nc.vector.tensor_reduce(stats[:, q * 16 + 14:q * 16 + 15], x_sb[:, q, :], AX.X, ALU.max),
                 reads=R(("x", q)), writes=R(("stc", q, 3)), tag=f"{tg}.mx{q}")
            b.op("dve", lambda q=q: nc.vector.tensor_reduce(stats[:, q * 16 + 15:q * 16 + 16], x_sb[:, q, :], AX.X, ALU.min),
                 reads=R(("x", q)), writes=R(("stc", q, 3)), tag=f"{tg}.mn{q}")

    def stats_fin(tg, q):
        b.op("dve", lambda q=q: nc.vector.tensor_scalar(col(q, "mean"), col(q, "sq"), 1.0 / D, None, ALU.mult),
             reads=R(("stc", q, 1)), writes=R(("stc", q, 2)), tag=f"{tg}.mm_{q}")
        b.op("dve", lambda q=q: nc.vector.scalar_tensor_tensor(col(q, "sq"), col(q, "mean"), 1.0, col(q, "mean"), ALU.mult, ALU.mult),
             reads=R(("stc", q, 2)), writes=R(("stc", q, 1)), tag=f"{tg}.m2{q}")
        b.op("dve", lambda q=q: nc.vector.scalar_tensor_tensor(col(q, "var"), col(q, "var"), 1.0 / D, col(q, "sq"), ALU.mult, ALU.subtract),
             reads=R(("stc", q, 1), ("stc", q, 2)), writes=R(("stc", q, 2)), tag=f"{tg}.var{q}")

    def batched_rstd(tg):
        b.op("act", lambda: nc.scalar.activation(stats[:, 8:64:16], stats[:, 13:64:16], AF.Sqrt, bias=eps_sb[:]),
             reads=R(("eps",), *[("stc", q, 2) for q in range(NQ)]),
             writes=R(*[("stc", q, 1) for q in range(NQ)]), tag=f"{tg}.sqv")
        b.op("dve", lambda: nc.vector.reciprocal(stats[:, 9:64:16], stats[:, 8:64:16]),
             reads=R(*[("stc", q, 1) for q in range(NQ)]),
             writes=R(*[("stc", q, 1) for q in range(NQ)]), tag=f"{tg}.rst")

    def quant_scalars(tg, q):
        b.op("dve", lambda q=q: nc.vector.tensor_tensor(col(q, "amax"), col(q, "mx"), col(q, "mean"), ALU.subtract),
             reads=R(("stc", q, 2)), writes=R(("stc", q, 0)), tag=f"{tg}.qa{q}")
        b.op("dve", lambda q=q: nc.vector.tensor_tensor(col(q, "mb"), col(q, "mean"), col(q, "mn"), ALU.subtract),
             reads=R(("stc", q, 2)), writes=R(("stc", q, 0)), tag=f"{tg}.qb{q}")
        b.op("dve", lambda q=q: nc.vector.tensor_tensor(col(q, "amax"), col(q, "amax"), col(q, "mb"), ALU.max),
             reads=R(("stc", q, 0)), writes=R(("stc", q, 0)), tag=f"{tg}.qm{q}")
        b.op("dve", lambda q=q: nc.vector.scalar_tensor_tensor(col(q, "clamp"), col(q, "amax"), 1.0, col(q, "rstd"), ALU.mult, ALU.mult),
             reads=R(("stc", q, 0), ("stc", q, 1)), writes=R(("stc", q, 0)), tag=f"{tg}.qs{q}")
        b.op("dve", lambda q=q: nc.vector.tensor_scalar(col(q, "clamp"), col(q, "clamp"), 1e-5, None, ALU.max),
             reads=R(("stc", q, 0)), writes=R(("stc", q, 0)), tag=f"{tg}.qc{q}")
        b.op("dve", lambda q=q: nc.vector.reciprocal(col(q, "inv"), col(q, "clamp")),
             reads=R(("stc", q, 0)), writes=R(("stc", q, 0)), tag=f"{tg}.qi{q}")
        b.op("dve", lambda q=q: nc.vector.scalar_tensor_tensor(col(q, "rs"), col(q, "inv"), 127.0, col(q, "rstd"), ALU.mult, ALU.mult),
             reads=R(("stc", q, 0), ("stc", q, 1)), writes=R(("stc", q, 1)), tag=f"{tg}.qr{q}")
        b.op("dve", lambda q=q: nc.vector.scalar_tensor_tensor(col(q, "mbneg"), col(q, "mean"), -1.0, col(q, "rs"), ALU.mult, ALU.mult),
             reads=R(("stc", q, 2), ("stc", q, 1)), writes=R(("stc", q, 1)), tag=f"{tg}.qn{q}")
        b.op("dve", lambda q=q: nc.vector.scalar_tensor_tensor(col(q, "alpha"), col(q, "clamp"), 0.5 / 127.0, ws_sb[:], ALU.mult, ALU.mult),
             reads=R(("stc", q, 0), ("ws",)), writes=R(("stc", q, 0)), tag=f"{tg}.ql{q}")

    def transposes_to_fmT(tg, q, src_res, ppi):
        ppt = pp[ppi]
        pres = PPN[ppi]
        for j in range(8):
            bank = j // 4
            b.op("pe", lambda j=j, bank=bank, ppt=ppt: nc.tensor.matmul(
                ppt[:, bank * 512 + (j % 4) * P: bank * 512 + (j % 4 + 1) * P],
                stage_bf[:, j * P:(j + 1) * P], id_sb[:], start=True, stop=True),
                reads=R((src_res,), ("id",)), writes=R((pres, bank)), tag=f"{tg}.tr{q}{j}")
        b.op("act", lambda q=q, ppt=ppt: nc.scalar.activation(
            fmT[:, 0:8, q * P:(q + 1) * P], ppt[:].rearrange("p (j n) -> p j n", n=P), AF.Identity),
            reads=R((pres, 0), (pres, 1)), writes=R(("fmT", q)), tag=f"{tg}.trE{q}")

    def stage_ffn(sc, half):
        tg = f"F{half}.{sc}"
        for q in range(NQ):
            stats_ln(tg, q, True)
        for q in range(NQ):
            stats_fin(tg, q)
        batched_rstd(tg)
        for q in range(NQ):
            quant_scalars(tg, q)
            b.op("dve", lambda q=q: nc.vector.tensor_scalar(xs[:], x_sb[:, q, :], col(q, "rs"),
                                                            col(q, "mbneg"), ALU.mult, ALU.add),
                 reads=R(("x", q), ("stc", q, 1)), writes=R(("xs",)), tag=f"{tg}.xs{q}")
            b.op("dve", lambda: nc.vector.tensor_copy(y1[:], xs[:]),
                 reads=R(("xs",)), writes=R(("y1",)), tag=f"{tg}.y1{q}")
            b.op("dve", lambda: nc.vector.scalar_tensor_tensor(ct[:], xs[:], -0.5, y1[:], ALU.add, ALU.is_gt),
                 reads=R(("xs",), ("y1",)), writes=R(("ct",)), tag=f"{tg}.cp{q}")
            b.op("dve", lambda: nc.vector.scalar_tensor_tensor(tt_f[:], xs[:], 0.5, y1[:], ALU.add, ALU.is_lt),
                 reads=R(("xs",), ("y1",)), writes=R(("ttf", 0), ("ttf", 1)), tag=f"{tg}.cn{q}")
            b.op("dve", lambda: nc.vector.tensor_tensor(ct[:], ct[:], tt_f[:], ALU.subtract),
                 reads=R(("ct",), ("ttf", 0), ("ttf", 1)), writes=R(("ct",)), tag=f"{tg}.cr{q}")
            b.op("dve", lambda: nc.vector.tensor_tensor(stage_bf[:], ct[:], y1[:], ALU.add),
                 reads=R(("ct",), ("y1",)), writes=R(("stage",)), tag=f"{tg}.q8{q}")
            transposes_to_fmT(tg, q, "stage", 2 + (q % 2))
        for q in range(NQ):
            ppi = q % 2
            ppt = pp[ppi]
            pres = PPN[ppi]
            for n in range(2):
                for j in range(8):
                    b.op("pe", lambda q=q, n=n, j=j, ppt=ppt: nc.tensor.matmul(
                        ppt[:, n * 512:(n + 1) * 512], fmT[:, j, q * P:(q + 1) * P],
                        tern[:, j, n * 512:(n + 1) * 512], start=(j == 0), stop=(j == 7)),
                        reads=R(("fmT", q), *[("tern", oc) for oc in range(4 * n, 4 * n + 4)]),
                        writes=R((pres, n)), tag=f"{tg}.mm{q}{n}{j}")
            for n in range(2):
                b.op("dve", lambda q=q, n=n, ppt=ppt: nc.vector.scalar_tensor_tensor(
                    tt_f[:, n * 512:(n + 1) * 512], ppt[:, n * 512:(n + 1) * 512], col(q, "alpha"),
                    bias_sb[:, n * 512:(n + 1) * 512], ALU.mult, ALU.add),
                    reads=R((pres, n), ("stc", q, 0), ("bias",)), writes=R(("ttf", n)), tag=f"{tg}.ep{q}{n}")
            b.op("pool", lambda q=q: nc.gpsimd.tensor_tensor(x_sb[:, q, :], x_sb[:, q, :], tt_f[:], ALU.add),
                 reads=R(("ttf", 0), ("ttf", 1), ("x", q)), writes=R(("x", q)), tag=f"{tg}.rs{q}")
            if half == 2:
                b.dma("act", f"so{q}", lambda sc=sc, q=q: nc.scalar.dma_start(
                    out=out[sc * SC + q * P: sc * SC + (q + 1) * P, :], in_=x_sb[:, q, :]),
                    reads=R(("x", q)), writes=R(("o", sc, q)), tag=f"xst{sc}.{q}")
                if sc + 1 < NSC:
                    emit_x_load(sc + 1, q)

    def stage_ln2(sc):
        tg = f"L.{sc}"
        for q in range(NQ):
            stats_ln(tg, q, False)
        for q in range(NQ):
            stats_fin(tg, q)
        batched_rstd(tg)
        for q in range(NQ):
            b.op("dve", lambda q=q: nc.vector.scalar_tensor_tensor(
                col(q, "mbneg"), col(q, "mean"), -1.0, col(q, "rstd"), ALU.mult, ALU.mult),
                reads=R(("stc", q, 2), ("stc", q, 1)), writes=R(("stc", q, 1)), tag=f"{tg}.mb{q}")
            b.op("dve", lambda q=q: nc.vector.tensor_scalar(stage_bf[:], x_sb[:, q, :], col(q, "rstd"),
                                                            col(q, "mbneg"), ALU.mult, ALU.add),
                 reads=R(("x", q), ("stc", q, 1)), writes=R(("stage",)), tag=f"{tg}.ap{q}")
            transposes_to_fmT(tg, q, "stage", 2 + (q % 2))

    def ktm_dst(h):
        return ktm[:].rearrange("p t (g n) -> p g t n", n=P)[:, 2 * h:2 * h + 2, :, :]

    def stage_qk(sc):
        tg = f"Q.{sc}"
        xs2 = xs[:].rearrange("p (a n) -> p a n", n=512)
        ct2 = ct[:].rearrange("p (a n) -> p a n", n=512)
        for h in range(H):
            if h + 1 < H:
                emit_rope_load(sc, h + 1)
            rts = rt_sb[h % 2] if mode == "linear" else rt_sb[0]
            rres = ("rt", h % 2) if mode == "linear" else ("rt", 0)
            for which, wname, dst, dres, tc, ts_ in ((0, "wq", qp, "qp", 0, 1),
                                                     (1, "wk", kp, "kp", 2, 3)):
                ppt = pp[which]
                pres = PPN[which]
                for cci in range(2):
                    dc = 2 * h + cci
                    for j in range(8):
                        s, g0, ncols = wslot(wname, j)
                        b.op("pe", lambda ppt=ppt, cci=cci, j=j, dc=dc, s=s, g0=g0, ncols=ncols: nc.tensor.matmul(
                            ppt[:, cci * 512:(cci + 1) * 512],
                            wg_sb[s][:].rearrange("p (c n) -> p c n", n=ncols)[:, j - g0, dc * P:(dc + 1) * P],
                            fmT[:, j, :], start=(j == 0), stop=(j == 7)),
                            reads=R(("wg", s), ("fmT", 0), ("fmT", 1), ("fmT", 2), ("fmT", 3)),
                            writes=R((pres, cci)), tag=f"{tg}.{wname}{dc}{j}")
                b.op("dve", lambda ppt=ppt, tc=tc, rts=rts: nc.vector.tensor_tensor(xs2[:, 0, :], ppt[:, 0:512], rts[:, tc, :], ALU.mult),
                     reads=R((pres, 0), rres), writes=R(("xs",)), tag=f"{tg}.ra{h}{which}")
                b.op("dve", lambda ppt=ppt, ts_=ts_, rts=rts: nc.vector.tensor_tensor(xs2[:, 1, :], ppt[:, 512:1024], rts[:, ts_, :], ALU.mult),
                     reads=R((pres, 1), rres), writes=R(("xs",)), tag=f"{tg}.rb{h}{which}")
                b.op("dve", lambda ppt=ppt, ts_=ts_, rts=rts: nc.vector.tensor_tensor(ct2[:, 0, :], ppt[:, 0:512], rts[:, ts_, :], ALU.mult),
                     reads=R((pres, 0), rres), writes=R(("ct",)), tag=f"{tg}.rc{h}{which}")
                b.op("dve", lambda ppt=ppt, tc=tc, rts=rts: nc.vector.tensor_tensor(ct2[:, 1, :], ppt[:, 512:1024], rts[:, tc, :], ALU.mult),
                     reads=R((pres, 1), rres), writes=R(("ct",)), tag=f"{tg}.rd{h}{which}")
                b.op("dve", lambda dst=dst, h=h: nc.vector.tensor_tensor(dst[:, 2 * h, :], xs2[:, 0, :], xs2[:, 1, :], ALU.subtract),
                     reads=R(("xs",)), writes=R((dres, 2 * h)), tag=f"{tg}.r0{h}{which}")
                b.op("dve", lambda dst=dst, h=h: nc.vector.tensor_tensor(dst[:, 2 * h + 1, :], ct2[:, 0, :], ct2[:, 1, :], ALU.add),
                     reads=R(("ct",)), writes=R((dres, 2 * h + 1)), tag=f"{tg}.r1{h}{which}")
            if mode == "linear":
                for cci in range(2):
                    for t in range(4):
                        b.op("pe", lambda cci=cci, t=t, h=h: nc.tensor.matmul(
                            pp[2][:, cci * 512 + t * P: cci * 512 + (t + 1) * P],
                            kp[:, 2 * h + cci, t * P:(t + 1) * P], id_sb[:], start=True, stop=True),
                            reads=R(("kp", 2 * h + cci), ("id",)), writes=R(("pp2", cci)), tag=f"{tg}.kt{h}{cci}{t}")
                b.op("act", lambda h=h: nc.scalar.activation(
                    ktm_dst(h), pp[2][:].rearrange("p (g t n) -> p g t n", t=4, n=P),
                    AF.Identity, scale=_gch()[h]),
                    reads=R(("pp2", 0), ("pp2", 1)), writes=R(("ktm", h)), tag=f"{tg}.ktE{h}")

    def store_kv_hist(sc):
        b.dma("act", "skst", lambda sc=sc: nc.scalar.dma_start(
            out=scr["k_hist"][sc], in_=kp[:]),
            reads=R(*[("kp", i) for i in range(8)]), writes=R(("d.kh", sc)), tag=f"kst{sc}")

    def store_v_hist(sc):
        b.dma("act", "svst", lambda sc=sc: nc.scalar.dma_start(
            out=scr["v_hist"][sc], in_=v_sb[:]),
            reads=R(*[("v", q, i) for q in range(NQ) for i in range(2)]),
            writes=R(("d.vh", sc)), tag=f"vst{sc}")

    def stage_retention_quad(sc):
        tg = f"Rq.{sc}"
        SCB = [("pp2", 0), ("pp2", 1), ("pp3", 0), ("pp3", 1)]
        SPP = [pp[2], pp[2], pp[3], pp[3]]
        SOF = [0, 512, 0, 512]
        emitted_loads = set()

        def pair_loads(h, bi):
            if (h, bi) in emitted_loads or h >= H or bi > sc:
                return
            emitted_loads.add((h, bi))
            bb = sc - bi
            dlt = bi
            ping = bi % 2
            if bb < sc:
                b.dma("syncq", f"skh{ping}", lambda h=h, bb=bb, ping=ping: nc.sync.dma_start(
                    out=khb[:, ping, :, :], in_=scr["k_hist"][bb][:, 2 * h:2 * h + 2, :]),
                    reads=R(("d.kh", bb)), writes=R(("khb", ping)), tag=f"{tg}.lk{h}{bb}")
                b.dma("act", f"svh{ping}", lambda h=h, bb=bb, ping=ping: nc.scalar.dma_start(
                    out=vhb[:, ping, :, :], in_=scr["v_hist"][bb][:, :, h * 512:(h + 1) * 512]),
                    reads=R(("d.vh", bb)), writes=R(("vhb", ping)), tag=f"{tg}.lv{h}{bb}")
            b.dma("act", f"smk{ping}", lambda h=h, dlt=dlt, ping=ping: nc.scalar.dma_start(
                out=msk[:, ping, :, :], in_=A["c_qmask"][h, 4 * dlt:4 * dlt + 4].rearrange("m p n -> p m n")),
                reads=R(), writes=R(("msk", ping)), tag=f"{tg}.lm{h}{dlt}")

        for h in range(H):
            for bi, bb in enumerate(range(sc, -1, -1)):
                dlt = sc - bb
                ping = bi % 2
                ping2 = bi % 2
                pair_loads(h, bi)
                pair_loads(h, bi + 1)
                for jb in range(NQ):
                    spp, soff, sbank = SPP[jb], SOF[jb], SCB[jb]
                    for cci in range(2):
                        if bb == sc:
                            lhs = kp[:, 2 * h + cci, jb * P:(jb + 1) * P]
                            lres = ("kp", 2 * h + cci)
                        else:
                            lhs = khb[:, ping2, cci, jb * P:(jb + 1) * P]
                            lres = ("khb", ping2)
                        b.op("pe", lambda spp=spp, soff=soff, lhs=lhs, cci=cci, h=h: nc.tensor.matmul(
                            spp[:, soff:soff + 512], lhs, qp[:, 2 * h + cci, :],
                            start=(cci == 0), stop=(cci == 1)),
                            reads=R(lres, ("qp", 2 * h + cci)), writes=R(sbank), tag=f"{tg}.s{h}{bb}{jb}{cci}")
                    b.op("dve", lambda spp=spp, soff=soff, jb=jb, ping=ping: nc.vector.tensor_tensor(
                        sct[:, jb, :], spp[:, soff:soff + 512], msk[:, ping, 3 - jb, :], ALU.mult),
                        reads=R(sbank, ("msk", ping)), writes=R(("sct", jb)), tag=f"{tg}.e{h}{bb}{jb}")
                for jb in range(NQ):
                    for qb in range(NQ):
                        ppo = pp[qb // 2]
                        offo = (qb % 2) * 512
                        obank = (PPN[qb // 2], qb % 2)
                        if bb == sc:
                            vsrc = v_sb[:, jb, h * 512:(h + 1) * 512]
                            vres = ("v", jb, h // 2)
                        else:
                            vsrc = vhb[:, ping2, jb, :]
                            vres = ("vhb", ping2)
                        b.op("pe", lambda ppo=ppo, offo=offo, jb=jb, qb=qb, vsrc=vsrc, st=(bi == 0 and jb == 0), sp=(bi == sc and jb == NQ - 1): nc.tensor.matmul(
                            ppo[:, offo:offo + 512], sct[:, jb, qb * P:(qb + 1) * P], vsrc,
                            start=st, stop=sp),
                            reads=R(("sct", jb), vres), writes=R(obank), tag=f"{tg}.a{h}{bb}{jb}{qb}")
            pair_loads(h + 1, 0)
            pair_loads(h + 1, 1)
            for qb in range(NQ):
                ppt = pp[qb // 2]
                off = (qb % 2) * 512
                bank = (PPN[qb // 2], qb % 2)
                b.op("act", lambda ppt=ppt, off=off, h=h, qb=qb: nc.scalar.activation(
                    t2[:], ppt[:, off:off + 512], AF.Square, accum_out=acol(h, qb)),
                    reads=R(bank), writes=R(("sta", h, 0), ("t2",)), tag=f"{tg}.sq{h}{qb}")
            b.op("act", lambda h=h: nc.scalar.activation(arow(h, 80), arow(h, 64), AF.Sqrt,
                                                         bias=eps_sb[:], scale=1.0 / 512.0),
                 reads=R(("eps",), ("sta", h, 0)), writes=R(("sta", h, 1)), tag=f"{tg}.rq{h}")
            b.op("dve", lambda h=h: nc.vector.reciprocal(arow(h, 96), arow(h, 80)),
                 reads=R(("sta", h, 1)), writes=R(("sta", h, 2)), tag=f"{tg}.rr{h}")
            for qb in range(NQ):
                ppt = pp[qb // 2]
                off = (qb % 2) * 512
                bank = (PPN[qb // 2], qb % 2)
                b.op("dve", lambda ppt=ppt, off=off, h=h, qb=qb: nc.vector.tensor_scalar(
                    t1[:], ppt[:, off:off + 512], arc(h, qb), None, ALU.mult),
                    reads=R(bank, ("sta", h, 2)), writes=R(("t1",)), tag=f"{tg}.t1{h}{qb}")
                b.op("pool", lambda: nc.gpsimd.tensor_tensor(t2[:], t1[:], gnw_sb[:], ALU.mult),
                     reads=R(("t1",), ("gnw",)), writes=R(("t2",)), tag=f"{tg}.t2{h}{qb}")
                b.op("pool", lambda h=h, qb=qb: nc.gpsimd.tensor_tensor(
                    g_sb[:, qb, h * 512:(h + 1) * 512], t2[:], g_sb[:, qb, h * 512:(h + 1) * 512], ALU.mult),
                    reads=R(("t2",), ("g", qb, h // 2)), writes=R(("g", qb, h // 2)), tag=f"{tg}.og{h}{qb}")

    def stage_vg(sc, wname):
        tg = f"{wname}.{sc}"
        dst = v_sb if wname == "wv" else g_sb
        dres = "v" if wname == "wv" else "g"
        for q in range(NQ):
            for npair in range(2):
                ppt = pp[npair]
                pres = PPN[npair]
                for nh in range(2):
                    n = npair * 2 + nh
                    for j in range(8):
                        s, g0, ncols = wslot(wname, j)
                        b.op("pe", lambda ppt=ppt, nh=nh, n=n, j=j, q=q, s=s, g0=g0, ncols=ncols: nc.tensor.matmul(
                            ppt[:, nh * 512:(nh + 1) * 512], fmT[:, j, q * P:(q + 1) * P],
                            wg_sb[s][:].rearrange("p (c n) -> p c n", n=ncols)[:, j - g0, n * 512:(n + 1) * 512],
                            start=(j == 0), stop=(j == 7)),
                            reads=R(("wg", s), ("fmT", q)), writes=R((pres, nh)), tag=f"{tg}.m{q}{n}{j}")
                fn = AF.Identity if wname == "wv" else AF.Silu
                b.op("act", lambda ppt=ppt, q=q, npair=npair, dst=dst, fn=fn: nc.scalar.activation(
                    dst[:, q, npair * 1024:(npair + 1) * 1024], ppt[:], fn),
                    reads=R((pres, 0), (pres, 1)), writes=R((dres, q, npair)), tag=f"{tg}.e{q}{npair}")

    def stage_retention(sc):
        tg = f"R.{sc}"
        for h in range(H):
            for jb in range(NQ):
                nfree = SC - jb * P
                off = (jb % 2) * 512
                bank = ("pp3", jb % 2)
                for cci in range(2):
                    b.op("pe", lambda off=off, jb=jb, cci=cci, h=h, nfree=nfree: nc.tensor.matmul(
                        pp[3][:, off:off + nfree], kp[:, 2 * h + cci, jb * P:(jb + 1) * P],
                        qp[:, 2 * h + cci, jb * P:SC], start=(cci == 0), stop=(cci == 1)),
                        reads=R(("kp", 2 * h + cci), ("qp", 2 * h + cci)), writes=R(bank), tag=f"{tg}.s{h}{jb}{cci}")
                b.op("dve", lambda off=off, jb=jb, nfree=nfree: nc.vector.tensor_tensor(
                    sct[:, jb, 0:nfree], pp[3][:, off:off + nfree], tri_sb[:, 0:nfree], ALU.mult),
                    reads=R(bank, ("tri",)), writes=R(("sct", jb)), tag=f"{tg}.sE{h}{jb}")
            for qb in range(NQ):
                ppi = qb // 2
                ppt = pp[ppi]
                off = (qb % 2) * 512
                bank = (PPN[ppi], qb % 2)
                for jb in range(qb + 1):
                    b.op("pe", lambda ppt=ppt, off=off, jb=jb, qb=qb, h=h: nc.tensor.matmul(
                        ppt[:, off:off + 512], sct[:, jb, (qb - jb) * P:(qb - jb + 1) * P],
                        v_sb[:, jb, h * 512:(h + 1) * 512], start=(jb == 0), stop=False),
                        reads=R(("sct", jb), ("v", jb, h // 2)), writes=R(bank), tag=f"{tg}.i{h}{qb}{jb}")
                for cci in range(2):
                    b.op("pe", lambda ppt=ppt, off=off, cci=cci, qb=qb, h=h: nc.tensor.matmul(
                        ppt[:, off:off + 512], qp[:, 2 * h + cci, qb * P:(qb + 1) * P],
                        scr_s[:, 2 * h + cci, :], start=False, stop=(cci == 1)),
                        reads=R(("qp", 2 * h + cci), ("scr", h, cci)), writes=R(bank), tag=f"{tg}.x{h}{qb}{cci}")
                b.op("act", lambda ppt=ppt, off=off, h=h, qb=qb: nc.scalar.activation(
                    t2[:], ppt[:, off:off + 512], AF.Square, accum_out=acol(h, qb)),
                    reads=R(bank), writes=R(("sta", h, 0), ("t2",)), tag=f"{tg}.sq{h}{qb}")
            b.op("act", lambda h=h: nc.scalar.activation(arow(h, 80), arow(h, 64), AF.Sqrt,
                                                         bias=eps_sb[:], scale=1.0 / 512.0),
                 reads=R(("eps",), ("sta", h, 0)), writes=R(("sta", h, 1)), tag=f"{tg}.rq{h}")
            b.op("dve", lambda h=h: nc.vector.reciprocal(arow(h, 96), arow(h, 80)),
                 reads=R(("sta", h, 1)), writes=R(("sta", h, 2)), tag=f"{tg}.rr{h}")
            for qb in range(NQ):
                ppi = qb // 2
                ppt = pp[ppi]
                off = (qb % 2) * 512
                bank = (PPN[ppi], qb % 2)
                b.op("dve", lambda ppt=ppt, off=off, h=h, qb=qb: nc.vector.tensor_scalar(
                    t1[:], ppt[:, off:off + 512], arc(h, qb), None, ALU.mult),
                    reads=R(bank, ("sta", h, 2)), writes=R(("t1",)), tag=f"{tg}.t1{h}{qb}")
                b.op("pool", lambda: nc.gpsimd.tensor_tensor(t2[:], t1[:], gnw_sb[:], ALU.mult),
                     reads=R(("t1",), ("gnw",)), writes=R(("t2",)), tag=f"{tg}.t2{h}{qb}")
                b.op("pool", lambda h=h, qb=qb: nc.gpsimd.tensor_tensor(
                    g_sb[:, qb, h * 512:(h + 1) * 512], t2[:], g_sb[:, qb, h * 512:(h + 1) * 512], ALU.mult),
                    reads=R(("t2",), ("g", qb, h // 2)), writes=R(("g", qb, h // 2)), tag=f"{tg}.og{h}{qb}")
            for cci in range(2):
                b.op("pe", lambda cci=cci, h=h: nc.tensor.matmul(
                    pp[2][:, cci * 512:(cci + 1) * 512], id_sb[:], u_s[:, 2 * h + cci, :],
                    start=True, stop=False),
                    reads=R(("id",), ("u", h, cci)), writes=R(("pp2", cci)), tag=f"{tg}.up{h}{cci}")
            for t in range(4):
                for cci in range(2):
                    b.op("pe", lambda t=t, cci=cci, h=h: nc.tensor.matmul(
                        pp[2][:, cci * 512:(cci + 1) * 512], ktm[:, t, (2 * h + cci) * P:(2 * h + cci + 1) * P],
                        v_sb[:, t, h * 512:(h + 1) * 512], start=False, stop=(t == 3)),
                        reads=R(("ktm", h), ("v", t, h // 2)), writes=R(("pp2", cci)), tag=f"{tg}.u{h}{t}{cci}")
            for cci in range(2):
                b.op("dve", lambda cci=cci, h=h: nc.vector.tensor_copy(
                    scr_s[:, 2 * h + cci, :], pp[2][:, cci * 512:(cci + 1) * 512]),
                    reads=R(("pp2", cci)), writes=R(("scr", h, cci)), tag=f"{tg}.cE{h}{cci}")
                b.op("act", lambda cci=cci, h=h: nc.scalar.activation(
                    u_s[:, 2 * h + cci, :], pp[2][:, cci * 512:(cci + 1) * 512], AF.Identity, scale=_gch()[h]),
                    reads=R(("pp2", cci)), writes=R(("u", h, cci)), tag=f"{tg}.uE{h}{cci}")

    def stage_op(sc):
        tg = f"O.{sc}"
        for q in range(NQ):
            for d in range(16):
                ppt = pp[2] if d < 8 else pp[3]
                pres = PPN[2] if d < 8 else PPN[3]
                dd = d % 8
                bankh = dd // 4
                b.op("pe", lambda ppt=ppt, dd=dd, d=d, q=q, bankh=bankh: nc.tensor.matmul(
                    ppt[:, bankh * 512 + (dd % 4) * P: bankh * 512 + (dd % 4 + 1) * P],
                    g_sb[:, q, d * P:(d + 1) * P], id_sb[:], start=True, stop=True),
                    reads=R(("g", q, d // 8), ("id",)), writes=R((pres, bankh)), tag=f"{tg}.t{q}{d}")
            b.op("act", lambda: nc.scalar.activation(ogT[:, 0:8, :], pp[2][:].rearrange("p (j n) -> p j n", n=P), AF.Identity),
                 reads=R(("pp2", 0), ("pp2", 1)), writes=R(("ogT", 0)), tag=f"{tg}.tEa{q}")
            b.op("act", lambda: nc.scalar.activation(ogT[:, 8:16, :], pp[3][:].rearrange("p (j n) -> p j n", n=P), AF.Identity),
                 reads=R(("pp3", 0), ("pp3", 1)), writes=R(("ogT", 1)), tag=f"{tg}.tEb{q}")
            ppi = q % 2
            ppt = pp[ppi]
            pres = PPN[ppi]
            for n in range(2):
                for d in range(16):
                    s, g0, ncols = wslot("wo", d)
                    b.op("pe", lambda ppt=ppt, n=n, d=d, s=s, g0=g0, ncols=ncols: nc.tensor.matmul(
                        ppt[:, n * 512:(n + 1) * 512], ogT[:, d, :],
                        wg_sb[s][:].rearrange("p (c n) -> p c n", n=ncols)[:, d - g0, n * 512:(n + 1) * 512],
                        start=(d == 0), stop=(d == 15)),
                        reads=R(("ogT", d // 8), ("wg", s)), writes=R((pres, n)), tag=f"{tg}.m{q}{n}{d}")
            for n in range(2):
                b.op("dve", lambda ppt=ppt, q=q, n=n: nc.vector.tensor_tensor(
                    x_sb[:, q, n * 512:(n + 1) * 512], ppt[:, n * 512:(n + 1) * 512],
                    x_sb[:, q, n * 512:(n + 1) * 512], ALU.add),
                    reads=R((pres, n), ("x", q)), writes=R(("x", q)), tag=f"{tg}.r{q}{n}")

    bar_res = ([("vstage",), ("gstage",)]
               + [("v", q, npp) for q in range(NQ) for npp in range(2)]
               + [("g", q, npp) for q in range(NQ) for npp in range(2)]
               + [("gstage", k) for k in range(8)])
    b.op("pool", lambda: nc.gpsimd.memset(t1[:, 0:2], 0.0), writes=bar_res, tag="prepbarB")

    def dump_dram(nm, src_ap, reads, tag):
        if not dbg:
            return
        b.dma("pool", "sdbg", lambda nm=nm, src_ap=src_ap: nc.gpsimd.dma_start(out=DBG[nm], in_=src_ap),
              reads=reads, writes=R(("dbg", nm), ("dbgchain",)), tag="dumpd:" + tag)

    def dump(nm, src_ap, reads, tag, tok_major=False):
        if not dbg:
            return
        dst = DBG[nm].rearrange("(q p) d -> p q d", p=P) if tok_major else DBG[nm]
        b.dma("pool", "sdbg", lambda dst=dst, src_ap=src_ap: nc.gpsimd.dma_start(out=dst, in_=src_ap),
              reads=reads, writes=R(("dbg", nm), ("dbgchain",)), tag="dump:" + tag)

    for q in range(NQ):
        emit_x_load(0, q)
    for sc in range(NSC):
        if sc > 0:
            emit_granule_loads(sc, ("wq", "wk"))
            emit_rope_load(sc, 0)
        if sc == 0:
            dump_dram("d_wq", scr["wq_wt"], R(*[("d.wq.wt", i) for i in range(NWT["wq"])]), "wq")
            dump_dram("d_wv", scr["wv_wt"], R(*[("d.wv.wt", i) for i in range(NWT["wv"])]), "wv")
        stage_ffn(sc, 1)
        if sc == 0:
            dump("d_x1", x_sb[:], R(*[("x", q) for q in range(NQ)]), "x1", tok_major=True)
            dump("d_tern", tern[:], R(*[("tern", oc) for oc in range(8)]), "tern")
        stage_ln2(sc)
        if sc == 0:
            dump("d_fmT", fmT[:], R(*[("fmT", q) for q in range(NQ)]), "fmT")
        stage_qk(sc)
        if sc == 0:
            dump("d_qp", qp[:], R(*[("qp", i) for i in range(8)]), "qp")
            dump("d_kp", kp[:], R(*[("kp", i) for i in range(8)]), "kp")
            if mode == "linear":
                dump("d_ktm", ktm[:], R(*[("ktm", h) for h in range(H)]), "ktm")
        if mode == "quad":
            store_kv_hist(sc)
        emit_granule_loads(sc, ("wv",))
        stage_vg(sc, "wv")
        if mode == "quad":
            store_v_hist(sc)
        emit_granule_loads(sc, ("wg",))
        stage_vg(sc, "wg")
        if sc == 0:
            dump("d_v", v_sb[:], R(*[("v", q, i) for q in range(NQ) for i in range(2)]), "v")
            dump("d_g", g_sb[:], R(*[("g", q, i) for q in range(NQ) for i in range(2)]), "g")
        emit_granule_loads(sc, ("wo",))
        if mode == "linear":
            stage_retention(sc)
        else:
            stage_retention_quad(sc)
        if sc == 0:
            dump("d_og", g_sb[:], R(*[("g", q, i) for q in range(NQ) for i in range(2)]), "og")
            if mode == "linear":
                dump("d_s", scr_s[:], R(*[("scr", h, c) for h in range(H) for c in range(2)]), "s")
            dump("d_sct", sct[:], R(*[("sct", jb) for jb in range(NQ)]), "sct")
        stage_op(sc)
        if sc == 0:
            dump("d_x2", x_sb[:], R(*[("x", q) for q in range(NQ)]), "x2", tok_major=True)
        stage_ffn(sc, 2)
    b.op("syncq", lambda: nc.sync.nop(),
         reads=R(*[("o", sc, q) for sc in range(NSC) for q in range(NQ)]), tag="fin")

    b.finalize()

    with nc.Block() as block:
        @block.tensor
        def _(eng):
            b.emit(eng, "pe", sems)

        @block.vector
        def _(eng):
            b.emit(eng, "dve", sems)

        @block.scalar
        def _(eng):
            b.emit(eng, "act", sems)

        @block.gpsimd
        def _(eng):
            b.emit(eng, "pool", sems)

        @block.sync
        def _(eng):
            b.emit(eng, "syncq", sems)

    ctx.close()
    return nc


def kernel(**inputs):
    tabs = _host_tables()
    nc = build_program()
    x = np.ascontiguousarray(inputs["L0_attn_x"])
    base = {k: np.ascontiguousarray(v) for k, v in inputs.items() if k != "L0_attn_x"}
    base.update(tabs)
    in_maps = [dict(base, L0_attn_x=x[core]) for core in range(8)]
    res = run_bass_kernel_spmd(nc, in_maps, core_ids=list(range(8)))
    return np.stack([np.asarray(r["out"]) for r in res.results], axis=0)


# version 23
# speedup vs baseline: 1.1921x; 1.1921x over previous
import numpy as np
import ml_dtypes
import contextlib
import concourse.bass as bass
import concourse.mybir as mybir
from concourse.bass_utils import run_bass_kernel_spmd

F32 = mybir.dt.float32
BF16 = mybir.dt.bfloat16
I32 = mybir.dt.int32
AF = mybir.ActivationFunctionType
ALU = mybir.AluOpType
AX = mybir.AxisListType

T, D = 4096, 1024
H = 4
SC, NSC, NQ = 512, 8, 4
P = 128
EPS = 1e-5
GAMMA = [1.0 - 2.0 ** (-5 - h) for h in range(H)]


def _log_decay_f32():
    import jax
    import jax.numpy as jnp
    with jax.default_device(jax.devices("cpu")[0]):
        return np.asarray(jnp.log1p(-jnp.exp2(-5.0 - jnp.arange(H, dtype=jnp.float32))))


_LD = None


def _ld():
    global _LD
    if _LD is None:
        _LD = _log_decay_f32().astype(np.float64)
    return _LD


def _gch():
    return [float(np.float32(np.exp(_ld()[h] * SC))) for h in range(H)]
NWG = 5
ATTN_MODE = "linear"
ENGINES = ("pe", "dve", "act", "pool", "syncq")
_ENG_NC = {"syncq": "sync", "act": "scalar", "pool": "gpsimd", "dve": "vector", "pe": "tensor"}


class _Op:
    __slots__ = ("eng", "fn", "deps", "kept", "seq", "inc", "sem", "semval",
                 "clock_before", "tag", "is_dma")

    def __init__(self, eng, fn, seq, tag, is_dma, sem):
        self.eng, self.fn, self.seq, self.tag = eng, fn, seq, tag
        self.is_dma, self.sem = is_dma, sem
        self.deps, self.kept = [], []
        self.inc = False
        self.semval = None
        self.clock_before = None


class PB:

    def __init__(self):
        self.ops = {e: [] for e in ENGINES}
        self.order = []
        self.last_w = {}
        self.readers = {}
        self.dma_count = {}

    def op(self, eng, fn, reads=(), writes=(), tag="", dma_sem=None):
        o = _Op(eng, fn, len(self.ops[eng]), tag, dma_sem is not None, dma_sem)
        if dma_sem is not None:
            self.dma_count[dma_sem] = self.dma_count.get(dma_sem, 0) + 1
            o.semval = 16 * self.dma_count[dma_sem]
        deps = []
        for r in reads:
            w = self.last_w.get(r)
            if w is not None:
                deps.append(w)
        for r in writes:
            w = self.last_w.get(r)
            if w is not None:
                deps.append(w)
            deps.extend(self.readers.get(r, ()))
        seen = set()
        for d in deps:
            if d is o or id(d) in seen:
                continue
            seen.add(id(d))
            if d.eng == "pe" and eng == "pe" and not d.is_dma and not o.is_dma:
                continue
            o.deps.append(d)
        for r in reads:
            self.readers.setdefault(r, []).append(o)
        for r in writes:
            self.last_w[r] = o
            self.readers[r] = []
        self.ops[eng].append(o)
        self.order.append(o)
        return o

    def dma(self, eng, sem, fn, reads=(), writes=(), tag=""):
        return self.op(eng, fn, reads, writes, tag, dma_sem=sem)

    @staticmethod
    def _merge(a, b):
        for k, v in b.items():
            if a.get(k, -1) < v:
                a[k] = v

    def _cclock(self, d):
        c = dict(d.clock_before)
        c[d.sem if d.is_dma else d.eng] = d.semval if d.is_dma else d.seq
        return c

    def finalize(self):
        for o in self.order:
            prev = self.ops[o.eng][o.seq - 1] if o.seq > 0 else None
            clock = {} if prev is None else dict(prev.clock_before)
            if prev is not None:
                for d in prev.kept:
                    self._merge(clock, self._cclock(d))
            o.clock_before = clock
            for d in o.deps:
                dom = d.sem if d.is_dma else d.eng
                val = d.semval if d.is_dma else d.seq
                if clock.get(dom, -1) >= val:
                    continue
                o.kept.append(d)
                self._merge(clock, self._cclock(d))
                if not d.is_dma:
                    d.inc = True
        for e in ENGINES:
            c = 0
            for o in self.ops[e]:
                if o.is_dma:
                    o.inc = True
                elif o.inc:
                    c += 1
                    o.semval = c

    def emit(self, eng_obj, eng_name, sems):
        own = sems[eng_name]
        for o in self.ops[eng_name]:
            waits = {}
            for d in o.kept:
                key = d.sem if d.is_dma else d.eng
                val = d.semval if d.is_dma else (d.semval if d.inc else None)
                if val is None:
                    raise RuntimeError(f"{o.tag}: dep {d.tag} has no semval")
                if waits.get(key, -1) < val:
                    waits[key] = val
            for key in sorted(waits):
                eng_obj.wait_ge(sems[key], waits[key])
            instr = o.fn()
            if o.is_dma:
                instr.then_inc(sems[o.sem], 16)
            elif o.inc:
                instr.then_inc(own, 1)


_TBL_CACHE = {}


def _host_tables(mode=None):
    mode = ATTN_MODE if mode is None else mode
    if _TBL_CACHE:
        return dict(_TBL_CACHE)
    import jax
    import jax.numpy as jnp
    with jax.default_device(jax.devices("cpu")[0]):
        inv_freq = 1.0 / 10000.0 ** (jnp.arange(0, 256, 2, dtype=jnp.float32) / 256)
        freqs = jnp.arange(T, dtype=jnp.float32)[:, None] * inv_freq[None, :]
        cos = np.asarray(jnp.cos(freqs)).astype(np.float64).T
        sin = np.asarray(jnp.sin(freqs)).astype(np.float64).T
    ld = _ld()
    jj = np.arange(P)[:, None]
    cc_ = np.arange(SC)[None, :]
    tri = ((cc_ >= jj) | (cc_ >= P)).astype(np.float32)
    if mode == "linear":
        i_in = (np.arange(T) % SC).astype(np.float64)
        rope = np.zeros((H, 4, 128, T), dtype=np.float64)
        for h in range(H):
            gq = np.exp(ld[h] * i_in) / 16.0
            gk = np.exp(-ld[h] * i_in)
            rope[h, 0] = cos * gq[None, :]
            rope[h, 1] = sin * gq[None, :]
            rope[h, 2] = cos * gk[None, :]
            rope[h, 3] = sin * gk[None, :]
        _TBL_CACHE.update(dict(c_rope=np.ascontiguousarray(rope.astype(np.float32))))
    else:
        rope = np.stack([cos / 16.0, sin / 16.0, cos, sin]).astype(np.float32)
        _TBL_CACHE.update(dict(c_rope=np.ascontiguousarray(rope)))
        import jax
        import jax.numpy as jnp
        ldf32 = _log_decay_f32()
        with jax.default_device(jax.devices("cpu")[0]):
            dr = jnp.arange(0, 4096, dtype=jnp.float32)
            m1 = np.stack([np.asarray(jnp.exp(ldf32[h] * dr)) for h in range(H)])
        qmask = np.zeros((H, 32, P, SC), dtype=np.float32)
        dd = (np.arange(SC)[None, :] - np.arange(P)[:, None])
        for h in range(H):
            for k0 in range(32):
                d = (k0 - 3) * P + dd
                valid = d >= 0
                qmask[h, k0][valid] = m1[h][d[valid]]
        _TBL_CACHE.update(dict(c_qmask=np.ascontiguousarray(qmask)))
    _TBL_CACHE.update(dict(
        c_tri=np.ascontiguousarray(tri),
        c_id=np.eye(P, dtype=ml_dtypes.bfloat16),
        c_ones=np.ones((1, P), dtype=np.float32),
    ))
    return dict(_TBL_CACHE)


def build_program(dbg=False, mode=None):
    mode = ATTN_MODE if mode is None else mode
    global NWG
    NWG = 4 if mode == "linear" else 5
    nc = bass.Bass("TRN2", target_bir_lowering=False)
    dt_in = {
        "L0_attn_x": ([T, D], F32), "L0_attn_Wq": ([D, D], F32),
        "L0_attn_Wk": ([D, D], F32), "L0_attn_Wv": ([2048, D], F32),
        "L0_attn_Wg": ([2048, D], F32), "L0_attn_Wo": ([D, 2048], F32),
        "L0_attn_g_norm_w": ([512], F32), "L0_ffn_packed_weight": ([262144], I32),
        "L0_ffn_weight_scale": ([1], F32), "L0_ffn_bias": ([D], F32),
        "c_rope": ([H, 4, 128, T] if mode == "linear" else [4, 128, T], F32),
        "c_tri": ([P, SC], F32),
        "c_id": ([P, P], BF16), "c_ones": ([1, P], F32),
    }
    if mode == "quad":
        dt_in["c_qmask"] = ([H, 32, P, SC], F32)
    A = {k: nc.dram_tensor(k, sh, dt, kind="ExternalInput").ap() for k, (sh, dt) in dt_in.items()}
    out = nc.dram_tensor("out", [T, D], F32, kind="ExternalOutput").ap()
    DBG = {}
    if dbg:
        for nm, sh, dt in [("d_x1", [SC, D], F32), ("d_x2", [SC, D], F32),
                           ("d_qp", [P, 8, SC], F32), ("d_kp", [P, 8, SC], F32),
                           ("d_ktm", [P, NQ, D], F32), ("d_v", [P, NQ, 2048], F32),
                           ("d_g", [P, NQ, 2048], F32), ("d_og", [P, NQ, 2048], F32),
                           ("d_s", [P, 8, 512], F32), ("d_sct", [P, NQ, SC], F32),
                           ("d_fmT", [P, 8, SC], F32), ("d_tern", [P, 8, D], F32),
                           ("d_q8", [P, 8, SC], F32), ("d_wq", [D, D], F32),
                           ("d_wv", [D, 2048], F32), ("d_rt", [P, 4, SC], F32),
                           ("d_wg0", [P, 4096], F32)]:
            DBG[nm] = nc.dram_tensor(nm, sh, dt, kind="ExternalOutput").ap()
    scr = {}
    for nm, sh in [("wq_bf", [D, D]), ("wk_bf", [D, D]), ("wv_bf", [2048, D]),
                   ("wg_bf", [2048, D]), ("wo_bf", [D, 2048]),
                   ("wq_wt", [D, D]), ("wk_wt", [D, D]), ("wv_wt", [D, 2048]),
                   ("wg_wt", [D, 2048]), ("wo_wt", [2048, D])]:
        scr[nm] = nc.dram_tensor(nm, sh, BF16, kind="Internal").ap()
    if mode == "quad":
        scr["k_hist"] = nc.dram_tensor("k_hist", [NSC, P, 8, 512], BF16, kind="Internal").ap()
        scr["v_hist"] = nc.dram_tensor("v_hist", [NSC, P, NQ, 2048], BF16, kind="Internal").ap()

    ctx = contextlib.ExitStack()
    sb = lambda nm, shape, dt: ctx.enter_context(nc.sbuf_tensor(nm, shape, dt))
    pst = lambda nm, shape: ctx.enter_context(nc.psum_tensor(nm, shape, F32))
    mksem = lambda nm: ctx.enter_context(nc.semaphore(nm))

    x_sb = sb("x_sb", [P, NQ, D], F32)
    fmT = sb("fmT", [P, 8, SC], BF16)
    qp = sb("qp", [P, 8, SC], BF16)
    kp = sb("kp", [P, 8, SC], BF16)
    ktm = sb("ktm", [P, NQ, D], BF16) if mode == "linear" else None
    v_sb = sb("v_sb", [P, NQ, 2048], BF16)
    g_sb = sb("g_sb", [P, NQ, 2048], BF16)
    ogT = sb("ogT", [P, 16, P], BF16)
    sct = sb("sct", [P, NQ, SC], BF16)
    if mode == "linear":
        scr_s = sb("scr_s", [P, 8, 512], BF16)
        u_s = sb("u_s", [P, 8, 512], BF16)
    else:
        scr_s = u_s = None
        msk = sb("msk", [P, 2, 4, SC], F32)
        khb = sb("khb", [P, 2, 2, SC], BF16)
        vhb = sb("vhb", [P, 2, NQ, SC], BF16)
    wg_sb = [sb(f"wgs{i}", [P, 4096], BF16) for i in range(NWG)]
    tern = sb("tern", [P, 8, D], BF16)
    rt_sb = ([sb("rts0", [P, 4, SC], F32), sb("rts1", [P, 4, SC], F32)]
             if mode == "linear" else [sb("rts0", [P, 4, SC], F32)])
    tri_sb = sb("tri_sb", [P, SC], F32)
    id_sb = sb("id_sb", [P, P], BF16)
    ones_sb = sb("ones_sb", [1, P], F32)
    bias_sb = sb("bias_sb", [P, D], F32)
    gnw_sb = sb("gnw_sb", [P, 512], F32)
    ws_sb = sb("ws_sb", [P, 1], F32)
    unp_i32 = sb("unp_i32", [P, D], I32)
    stats = sb("stats", [P, 112], F32)
    xs = sb("xs", [P, D], F32)
    y1 = sb("y1", [P, D], I32)
    ct = sb("ct", [P, D], F32)
    stage_bf = sb("stage_bf", [P, D], BF16)
    t1 = sb("t1", [P, 512], F32)
    t2 = sb("t2", [P, 512], F32)
    tt_f = sb("tt_f", [P, D], F32)
    eps_sb = sb("eps_sb", [P, 1], F32)

    pp = [pst(f"pp{i}", [P, 1024]) for i in range(4)]
    PPN = ["pp0", "pp1", "pp2", "pp3"]

    sems = {e: mksem("sm_" + e) for e in ENGINES}
    for nm in (["sx%d" % q for q in range(NQ)] + ["so%d" % q for q in range(NQ)]
               + ["swg%d" % i for i in range(NWG)] + ["srt0", "srt1", "scst", "pc0", "pc1",
                  "ps0", "ps1", "ps2", "ps3", "sid", "sones", "srow", "spk0", "spk1", "sdbg",
                  "skh0", "skh1", "svh0", "svh1", "smk0", "smk1", "skst", "svst"]):
        sems[nm] = mksem("sm_" + nm)

    b = PB()
    R = lambda *a: list(a)

    def ld(eng, sem, dst_ap, src_ap, writes, reads=(), tag="ld"):
        nceng = getattr(nc, _ENG_NC[eng])
        b.dma(eng, sem, lambda d=dst_ap, s=src_ap, e=nceng: e.dma_start(out=d, in_=s),
              reads=reads, writes=writes, tag=tag)

    CM = dict(amax=0, clamp=1, inv=2, alpha=3, mb=4, sq=8, rstd=9, mbneg=10,
              rs=11, mean=12, var=13, mx=14, mn=15)

    def col(q, name):
        c = q * 16 + CM[name]
        return stats[:, c:c + 1]

    def acol(h, qb):
        return stats[:, 64 + 4 * h + qb: 65 + 4 * h + qb]

    def arow(h, base):
        return stats[:, base + 4 * h: base + 4 * h + 4]

    def arc(h, qb):
        return stats[:, 96 + 4 * h + qb: 97 + 4 * h + qb]

    ld("syncq", "scst", tri_sb[:], A["c_tri"], R(("tri",)))
    ld("syncq", "sid", id_sb[:], A["c_id"], R(("id",)))
    ld("syncq", "sones", ones_sb[:], A["c_ones"], R(("ones",)))
    ld("syncq", "srow", bias_sb[:1, 0:D], A["L0_ffn_bias"][None, :], R(("rowb",)))
    for n in range(2):
        b.op("pe", lambda n=n: nc.tensor.matmul(pp[0][:, n * 512:(n + 1) * 512], ones_sb[:],
                                                bias_sb[:1, n * 512:(n + 1) * 512], start=True, stop=True),
             reads=R(("rowb",), ("ones",)), writes=R(("pp0", n)), tag="pbias%d" % n)
    b.op("act", lambda: nc.scalar.activation(bias_sb[:], pp[0][:], AF.Identity, scale=0.5),
         reads=R(("pp0", 0), ("pp0", 1)), writes=R(("bias",), ("rowb",)), tag="pbiasE")
    ld("syncq", "srow", gnw_sb[:1, 0:512], A["L0_attn_g_norm_w"][None, :], R(("rowg",)), reads=R(("rowb",)))
    b.op("pe", lambda: nc.tensor.matmul(pp[0][:, 0:512], ones_sb[:], gnw_sb[:1, 0:512], start=True, stop=True),
         reads=R(("rowg",), ("ones",), ("bias",)), writes=R(("pp0", 0)), tag="pgnw")
    b.op("act", lambda: nc.scalar.activation(gnw_sb[:], pp[0][:, 0:512], AF.Identity),
         reads=R(("pp0", 0),), writes=R(("gnw",), ("rowg",)), tag="pgnwE")
    ld("syncq", "srow", ws_sb[:1, 0:1], A["L0_ffn_weight_scale"][None, :], R(("roww",)), reads=R(("rowg",)))
    b.op("pe", lambda: nc.tensor.matmul(pp[0][:, 0:1], ones_sb[:], ws_sb[:1, 0:1], start=True, stop=True),
         reads=R(("roww",), ("ones",), ("gnw",)), writes=R(("pp0", 0)), tag="pws")
    b.op("act", lambda: nc.scalar.activation(ws_sb[:], pp[0][:, 0:1], AF.Identity),
         reads=R(("pp0", 0),), writes=R(("ws",), ("roww",)), tag="pwsE")
    b.op("pool", lambda: nc.gpsimd.memset(eps_sb[:], EPS), writes=R(("eps",)), tag="zE")
    if dbg:
        b.op("pool", lambda: nc.gpsimd.memset(sct[:], 0.0), writes=R(*[("sct", jb) for jb in range(NQ)]), tag="zsct")
    if mode == "linear":
        b.op("pool", lambda: nc.gpsimd.memset(scr_s[:], 0.0),
             writes=R(*[("scr", h, c) for h in range(H) for c in range(2)]), tag="zS")
        b.op("pool", lambda: nc.gpsimd.memset(u_s[:], 0.0),
             writes=R(*[("u", h, c) for h in range(H) for c in range(2)]), tag="zU")

    pk = A["L0_ffn_packed_weight"].rearrange("(c p n) -> c p n", p=P, n=D)
    ld("pool", "spk0", y1[:], pk[0], R(("pky",)), tag="pkl0")
    first_chunk1 = [True]
    for pc in (0, 1):
        if pc == 1:
            ld("pool", "spk0", y1[:], pk[1], R(("pky",)), tag="pkl1")
        for oc in range(pc, 8, 2):
            sh = 2 * (oc // 2)
            b.op("dve", lambda sh=sh: nc.vector.tensor_scalar(
                unp_i32[:], y1[:], sh, 3, ALU.logical_shift_right, ALU.bitwise_and),
                reads=R(("pky",),), writes=R(("unp",)), tag=f"unp{oc}")
            b.op("dve", lambda: nc.vector.tensor_scalar(stage_bf[:], unp_i32[:], -1.0, None, ALU.add),
                 reads=R(("unp",),), writes=R(("stage",)), tag=f"unpb{oc}")
            for j in range(8):
                bank = j // 4
                b.op("pe", lambda j=j, bank=bank: nc.tensor.matmul(
                    pp[2][:, bank * 512 + (j % 4) * P: bank * 512 + (j % 4 + 1) * P],
                    stage_bf[:, j * P:(j + 1) * P], id_sb[:], start=True, stop=True),
                    reads=R(("stage",), ("id",)), writes=R(("pp2", bank)), tag=f"tT{oc}{j}")
            b.op("act", lambda oc=oc: nc.scalar.activation(
                tern[:, 0:8, oc * P:(oc + 1) * P],
                pp[2][:].rearrange("p (j n) -> p j n", n=P), AF.Identity),
                reads=R(("pp2", 0), ("pp2", 1)), writes=R(("tern", oc)), tag=f"tE{oc}")

    b.op("pool", lambda: nc.gpsimd.memset(t2[:, 0:2], 0.0),
         writes=[("pky",), ("unp",), ("y1",), ("ct",), ("xs",), ("stage",),
                 ("ttf", 0), ("ttf", 1)], tag="prepbarA")

    v_all = v_sb[:].rearrange("p a (s n) -> p (a s) n", n=D)
    v_all2 = v_sb[:]

    def prep_weight(wname, src, bf, wt, rows, cols, pci, psi):
        nch = rows // P
        seg = src.rearrange("(c p) i -> c p i", p=P)
        bfr = bf.rearrange("(c p) i -> c p i", p=P)
        per_cast = 8 if cols == D else 4
        nh = nch // per_cast
        vv = v_all if cols == D else v_all2
        for hh in range(nh):
            b.dma("pool", f"pc{pci}", lambda hh=hh, seg=seg, vv=vv, per_cast=per_cast: nc.gpsimd.dma_start(
                out=vv[:, 0:per_cast, :],
                in_=seg[hh * per_cast:(hh + 1) * per_cast].rearrange("c p i -> p c i")),
                reads=R(), writes=R(("vstage",)), tag=f"cast:{wname}{hh}")
            b.dma("syncq", f"ps{psi}", lambda hh=hh, bfr=bfr, vv=vv, per_cast=per_cast: nc.sync.dma_start(
                out=bfr[hh * per_cast:(hh + 1) * per_cast].rearrange("c p i -> p c i"),
                in_=vv[:, 0:per_cast, :]),
                reads=R(("vstage",)), writes=R((f"d.{wname}.bf", hh)), tag=f"stbf:{wname}{hh}")
        ncc = cols // P
        per_store = 8192 // rows
        g_view = g_sb[:] if rows == 2048 else g_sb[:].rearrange("p a (s n) -> p (a s) n", n=rows)
        wtr = wt.rearrange("(c p) o -> c p o", p=P)
        done, st_i = 0, 0
        while done < ncc:
            cnt = min(per_store, ncc - done)
            for k in range(cnt):
                c = done + k
                b.dma("syncq", f"ps{psi}", lambda c=c, k=k, bf=bf, rows=rows, g_view=g_view: nc.sync.dma_start(
                    out=g_view[:, k, :], in_=bf[0:rows, c * P:(c + 1) * P], transpose=True),
                    reads=R(*[(f"d.{wname}.bf", hh) for hh in range(nh)], ("gstage",)),
                    writes=R(("gstage", k)), tag=f"tld:{wname}{c}")
            b.dma("syncq", f"ps{(psi + 2) % 4}", lambda done=done, cnt=cnt, wtr=wtr, g_view=g_view: nc.sync.dma_start(
                out=wtr[done:done + cnt].rearrange("c p o -> p c o"), in_=g_view[:, 0:cnt, :]),
                reads=R(*[("gstage", k) for k in range(cnt)]),
                writes=R(("gstage",), (f"d.{wname}.wt", st_i)), tag=f"stwt:{wname}{st_i}")
            done += cnt
            st_i += 1
        return st_i

    GR_PLAN = []
    slot = 0
    for wname, wt, nin_tot, ncols in (("wq", scr["wq_wt"], 8, D), ("wk", scr["wk_wt"], 8, D),
                                      ("wv", scr["wv_wt"], 8, 2048), ("wg", scr["wg_wt"], 8, 2048),
                                      ("wo", scr["wo_wt"], 16, D)):
        per = 4096 // ncols
        for g0 in range(0, nin_tot, per):
            GR_PLAN.append((slot % NWG, wname, wt, g0, per, ncols))
            slot += 1

    def wslot(wname, j):
        for (s, wn, wt, g0, per, ncols) in GR_PLAN:
            if wn == wname and g0 <= j < g0 + per:
                return s, g0, ncols
        raise KeyError((wname, j))

    def emit_granule_loads(sc, which):
        for gi, (s, wname, wt, g0, per, ncols) in enumerate(GR_PLAN):
            if wname not in which:
                continue
            wtr = wt.rearrange("(c p) o -> c p o", p=P)
            b.dma("syncq", f"swg{s}", lambda s=s, wtr=wtr, g0=g0, per=per, ncols=ncols: nc.sync.dma_start(
                out=wg_sb[s][:].rearrange("p (c n) -> p c n", n=ncols)[:, 0:per, :],
                in_=wtr[g0:g0 + per].rearrange("c p o -> p c o")),
                reads=R(*[(f"d.{wname}.wt", i) for i in range(NWT[wname])]),
                writes=R(("wg", s)), tag=f"gld{sc}:{wname}{g0}")
            if dbg and sc == 0 and wname == "wq" and g0 == 0:
                dump("d_wg0", wg_sb[0][:], R(("wg", 0)), "wg0")

    def emit_rope_load(sc, h):
        if mode == "quad":
            if h != 0:
                return
            b.dma("syncq", "srt0", lambda sc=sc: nc.sync.dma_start(
                out=rt_sb[0][:],
                in_=A["c_rope"][:, :, sc * SC:(sc + 1) * SC].rearrange("t p n -> p t n")),
                reads=R(), writes=R(("rt", 0)), tag=f"rtld{sc}")
            return
        b.dma("syncq", f"srt{h % 2}", lambda sc=sc, h=h: nc.sync.dma_start(
            out=rt_sb[h % 2][:],
            in_=A["c_rope"][h, :, :, sc * SC:(sc + 1) * SC].rearrange("t p n -> p t n")),
            reads=R(), writes=R(("rt", h % 2)), tag=f"rtld{sc}.{h}")
        if dbg and sc == 0 and h == 0:
            dump("d_rt", rt_sb[0][:], R(("rt", 0)), "rt0")

    def emit_x_load(sc, q):
        b.dma("act", f"sx{q}", lambda sc=sc, q=q: nc.scalar.dma_start(
            out=x_sb[:, q, :], in_=A["L0_attn_x"][sc * SC + q * P: sc * SC + (q + 1) * P, :]),
            reads=R(), writes=R(("x", q)), tag=f"xld{sc}.{q}")

    NWT = {}
    NWT["wq"] = prep_weight("wq", A["L0_attn_Wq"], scr["wq_bf"], scr["wq_wt"], D, D, 0, 0)
    NWT["wk"] = prep_weight("wk", A["L0_attn_Wk"], scr["wk_bf"], scr["wk_wt"], D, D, 1, 1)
    emit_granule_loads(0, ("wq", "wk"))
    emit_rope_load(0, 0)
    NWT["wv"] = prep_weight("wv", A["L0_attn_Wv"], scr["wv_bf"], scr["wv_wt"], 2048, D, 0, 0)
    NWT["wg"] = prep_weight("wg", A["L0_attn_Wg"], scr["wg_bf"], scr["wg_wt"], 2048, D, 1, 1)
    NWT["wo"] = prep_weight("wo", A["L0_attn_Wo"], scr["wo_bf"], scr["wo_wt"], D, 2048, 0, 0)

    def stats_ln(tg, q, want_minmax):
        b.op("act", lambda q=q: nc.scalar.activation(y1[:], x_sb[:, q, :], AF.Identity, accum_out=col(q, "sq")),
             reads=R(("x", q)), writes=R(("y1",), ("stc", q, 1)), tag=f"{tg}.sum{q}")
        b.op("act", lambda q=q: nc.scalar.activation(y1[:], x_sb[:, q, :], AF.Square, accum_out=col(q, "var")),
             reads=R(("x", q)), writes=R(("y1",), ("stc", q, 2)), tag=f"{tg}.ssq{q}")
        if want_minmax:
            b.op("dve", lambda q=q: nc.vector.tensor_reduce(stats[:, q * 16 + 14:q * 16 + 15], x_sb[:, q, :], AX.X, ALU.max),
                 reads=R(("x", q)), writes=R(("stc", q, 3)), tag=f"{tg}.mx{q}")
            b.op("dve", lambda q=q: nc.vector.tensor_reduce(stats[:, q * 16 + 15:q * 16 + 16], x_sb[:, q, :], AX.X, ALU.min),
                 reads=R(("x", q)), writes=R(("stc", q, 3)), tag=f"{tg}.mn{q}")

    def stats_fin(tg, q):
        b.op("dve", lambda q=q: nc.vector.tensor_scalar(col(q, "mean"), col(q, "sq"), 1.0 / D, None, ALU.mult),
             reads=R(("stc", q, 1)), writes=R(("stc", q, 2)), tag=f"{tg}.mm_{q}")
        b.op("dve", lambda q=q: nc.vector.scalar_tensor_tensor(col(q, "sq"), col(q, "mean"), 1.0, col(q, "mean"), ALU.mult, ALU.mult),
             reads=R(("stc", q, 2)), writes=R(("stc", q, 1)), tag=f"{tg}.m2{q}")
        b.op("dve", lambda q=q: nc.vector.scalar_tensor_tensor(col(q, "var"), col(q, "var"), 1.0 / D, col(q, "sq"), ALU.mult, ALU.subtract),
             reads=R(("stc", q, 1), ("stc", q, 2)), writes=R(("stc", q, 2)), tag=f"{tg}.var{q}")

    def batched_rstd(tg):
        b.op("act", lambda: nc.scalar.activation(stats[:, 8:64:16], stats[:, 13:64:16], AF.Sqrt, bias=eps_sb[:]),
             reads=R(("eps",), *[("stc", q, 2) for q in range(NQ)]),
             writes=R(*[("stc", q, 1) for q in range(NQ)]), tag=f"{tg}.sqv")
        b.op("dve", lambda: nc.vector.reciprocal(stats[:, 9:64:16], stats[:, 8:64:16]),
             reads=R(*[("stc", q, 1) for q in range(NQ)]),
             writes=R(*[("stc", q, 1) for q in range(NQ)]), tag=f"{tg}.rst")

    def quant_scalars(tg, q):
        b.op("dve", lambda q=q: nc.vector.tensor_tensor(col(q, "amax"), col(q, "mx"), col(q, "mean"), ALU.subtract),
             reads=R(("stc", q, 2)), writes=R(("stc", q, 0)), tag=f"{tg}.qa{q}")
        b.op("dve", lambda q=q: nc.vector.tensor_tensor(col(q, "mb"), col(q, "mean"), col(q, "mn"), ALU.subtract),
             reads=R(("stc", q, 2)), writes=R(("stc", q, 0)), tag=f"{tg}.qb{q}")
        b.op("dve", lambda q=q: nc.vector.tensor_tensor(col(q, "amax"), col(q, "amax"), col(q, "mb"), ALU.max),
             reads=R(("stc", q, 0)), writes=R(("stc", q, 0)), tag=f"{tg}.qm{q}")
        b.op("dve", lambda q=q: nc.vector.scalar_tensor_tensor(col(q, "clamp"), col(q, "amax"), 1.0, col(q, "rstd"), ALU.mult, ALU.mult),
             reads=R(("stc", q, 0), ("stc", q, 1)), writes=R(("stc", q, 0)), tag=f"{tg}.qs{q}")
        b.op("dve", lambda q=q: nc.vector.tensor_scalar(col(q, "clamp"), col(q, "clamp"), 1e-5, None, ALU.max),
             reads=R(("stc", q, 0)), writes=R(("stc", q, 0)), tag=f"{tg}.qc{q}")
        b.op("dve", lambda q=q: nc.vector.reciprocal(col(q, "inv"), col(q, "clamp")),
             reads=R(("stc", q, 0)), writes=R(("stc", q, 0)), tag=f"{tg}.qi{q}")
        b.op("dve", lambda q=q: nc.vector.scalar_tensor_tensor(col(q, "rs"), col(q, "inv"), 127.0, col(q, "rstd"), ALU.mult, ALU.mult),
             reads=R(("stc", q, 0), ("stc", q, 1)), writes=R(("stc", q, 1)), tag=f"{tg}.qr{q}")
        b.op("dve", lambda q=q: nc.vector.scalar_tensor_tensor(col(q, "mbneg"), col(q, "mean"), -1.0, col(q, "rs"), ALU.mult, ALU.mult),
             reads=R(("stc", q, 2), ("stc", q, 1)), writes=R(("stc", q, 1)), tag=f"{tg}.qn{q}")
        b.op("dve", lambda q=q: nc.vector.scalar_tensor_tensor(col(q, "alpha"), col(q, "clamp"), 0.5 / 127.0, ws_sb[:], ALU.mult, ALU.mult),
             reads=R(("stc", q, 0), ("ws",)), writes=R(("stc", q, 0)), tag=f"{tg}.ql{q}")

    def transposes_to_fmT(tg, q, src_res, ppi):
        ppt = pp[ppi]
        pres = PPN[ppi]
        for j in range(8):
            bank = j // 4
            b.op("pe", lambda j=j, bank=bank, ppt=ppt: nc.tensor.matmul(
                ppt[:, bank * 512 + (j % 4) * P: bank * 512 + (j % 4 + 1) * P],
                stage_bf[:, j * P:(j + 1) * P], id_sb[:], start=True, stop=True),
                reads=R((src_res,), ("id",)), writes=R((pres, bank)), tag=f"{tg}.tr{q}{j}")
        b.op("act", lambda q=q, ppt=ppt: nc.scalar.activation(
            fmT[:, 0:8, q * P:(q + 1) * P], ppt[:].rearrange("p (j n) -> p j n", n=P), AF.Identity),
            reads=R((pres, 0), (pres, 1)), writes=R(("fmT", q)), tag=f"{tg}.trE{q}")

    def stage_ffn(sc, half):
        tg = f"F{half}.{sc}"
        for q in range(NQ):
            stats_ln(tg, q, True)
        for q in range(NQ):
            stats_fin(tg, q)
        batched_rstd(tg)
        for q in range(NQ):
            quant_scalars(tg, q)
        for q in range(NQ):
            b.op("dve", lambda q=q: nc.vector.tensor_scalar(xs[:], x_sb[:, q, :], col(q, "rs"),
                                                            col(q, "mbneg"), ALU.mult, ALU.add),
                 reads=R(("x", q), ("stc", q, 1)), writes=R(("xs",)), tag=f"{tg}.xs{q}")
            b.op("dve", lambda: nc.vector.tensor_copy(y1[:], xs[:]),
                 reads=R(("xs",)), writes=R(("y1",)), tag=f"{tg}.y1{q}")
            b.op("dve", lambda: nc.vector.scalar_tensor_tensor(ct[:], xs[:], -0.5, y1[:], ALU.add, ALU.is_gt),
                 reads=R(("xs",), ("y1",)), writes=R(("ct",)), tag=f"{tg}.cp{q}")
            b.op("dve", lambda: nc.vector.scalar_tensor_tensor(tt_f[:], xs[:], 0.5, y1[:], ALU.add, ALU.is_lt),
                 reads=R(("xs",), ("y1",)), writes=R(("ttf", 0), ("ttf", 1)), tag=f"{tg}.cn{q}")
            b.op("dve", lambda: nc.vector.tensor_tensor(ct[:], ct[:], tt_f[:], ALU.subtract),
                 reads=R(("ct",), ("ttf", 0), ("ttf", 1)), writes=R(("ct",)), tag=f"{tg}.cr{q}")
            b.op("dve", lambda: nc.vector.tensor_tensor(stage_bf[:], ct[:], y1[:], ALU.add),
                 reads=R(("ct",), ("y1",)), writes=R(("stage",)), tag=f"{tg}.q8{q}")
            transposes_to_fmT(tg, q, "stage", 2 + (q % 2))
        for q in range(NQ):
            ppi = q % 2
            ppt = pp[ppi]
            pres = PPN[ppi]
            for n in range(2):
                for j in range(8):
                    b.op("pe", lambda q=q, n=n, j=j, ppt=ppt: nc.tensor.matmul(
                        ppt[:, n * 512:(n + 1) * 512], fmT[:, j, q * P:(q + 1) * P],
                        tern[:, j, n * 512:(n + 1) * 512], start=(j == 0), stop=(j == 7)),
                        reads=R(("fmT", q), *[("tern", oc) for oc in range(4 * n, 4 * n + 4)]),
                        writes=R((pres, n)), tag=f"{tg}.mm{q}{n}{j}")
            for n in range(2):
                b.op("dve", lambda q=q, n=n, ppt=ppt: nc.vector.scalar_tensor_tensor(
                    tt_f[:, n * 512:(n + 1) * 512], ppt[:, n * 512:(n + 1) * 512], col(q, "alpha"),
                    bias_sb[:, n * 512:(n + 1) * 512], ALU.mult, ALU.add),
                    reads=R((pres, n), ("stc", q, 0), ("bias",)), writes=R(("ttf", n)), tag=f"{tg}.ep{q}{n}")
            b.op("pool", lambda q=q: nc.gpsimd.tensor_tensor(x_sb[:, q, :], x_sb[:, q, :], tt_f[:], ALU.add),
                 reads=R(("ttf", 0), ("ttf", 1), ("x", q)), writes=R(("x", q)), tag=f"{tg}.rs{q}")
            if half == 2:
                b.dma("act", f"so{q}", lambda sc=sc, q=q: nc.scalar.dma_start(
                    out=out[sc * SC + q * P: sc * SC + (q + 1) * P, :], in_=x_sb[:, q, :]),
                    reads=R(("x", q)), writes=R(("o", sc, q)), tag=f"xst{sc}.{q}")
                if sc + 1 < NSC:
                    emit_x_load(sc + 1, q)

    def stage_ln2(sc):
        tg = f"L.{sc}"
        for q in range(NQ):
            stats_ln(tg, q, False)
        for q in range(NQ):
            stats_fin(tg, q)
        batched_rstd(tg)
        for q in range(NQ):
            b.op("dve", lambda q=q: nc.vector.scalar_tensor_tensor(
                col(q, "mbneg"), col(q, "mean"), -1.0, col(q, "rstd"), ALU.mult, ALU.mult),
                reads=R(("stc", q, 2), ("stc", q, 1)), writes=R(("stc", q, 1)), tag=f"{tg}.mb{q}")
            b.op("dve", lambda q=q: nc.vector.tensor_scalar(stage_bf[:], x_sb[:, q, :], col(q, "rstd"),
                                                            col(q, "mbneg"), ALU.mult, ALU.add),
                 reads=R(("x", q), ("stc", q, 1)), writes=R(("stage",)), tag=f"{tg}.ap{q}")
            transposes_to_fmT(tg, q, "stage", 2 + (q % 2))

    def ktm_dst(h):
        return ktm[:].rearrange("p t (g n) -> p g t n", n=P)[:, 2 * h:2 * h + 2, :, :]

    def stage_qk(sc):
        tg = f"Q.{sc}"
        xs2 = xs[:].rearrange("p (a n) -> p a n", n=512)
        ct2 = ct[:].rearrange("p (a n) -> p a n", n=512)
        for h in range(H):
            if h + 1 < H:
                emit_rope_load(sc, h + 1)
            rts = rt_sb[h % 2] if mode == "linear" else rt_sb[0]
            rres = ("rt", h % 2) if mode == "linear" else ("rt", 0)
            for which, wname, dst, dres, tc, ts_ in ((0, "wq", qp, "qp", 0, 1),
                                                     (1, "wk", kp, "kp", 2, 3)):
                ppt = pp[which]
                pres = PPN[which]
                for cci in range(2):
                    dc = 2 * h + cci
                    for j in range(8):
                        s, g0, ncols = wslot(wname, j)
                        b.op("pe", lambda ppt=ppt, cci=cci, j=j, dc=dc, s=s, g0=g0, ncols=ncols: nc.tensor.matmul(
                            ppt[:, cci * 512:(cci + 1) * 512],
                            wg_sb[s][:].rearrange("p (c n) -> p c n", n=ncols)[:, j - g0, dc * P:(dc + 1) * P],
                            fmT[:, j, :], start=(j == 0), stop=(j == 7)),
                            reads=R(("wg", s), ("fmT", 0), ("fmT", 1), ("fmT", 2), ("fmT", 3)),
                            writes=R((pres, cci)), tag=f"{tg}.{wname}{dc}{j}")
                b.op("dve", lambda ppt=ppt, tc=tc, rts=rts: nc.vector.tensor_tensor(xs2[:, 0, :], ppt[:, 0:512], rts[:, tc, :], ALU.mult),
                     reads=R((pres, 0), rres), writes=R(("xs",)), tag=f"{tg}.ra{h}{which}")
                b.op("dve", lambda ppt=ppt, ts_=ts_, rts=rts: nc.vector.tensor_tensor(xs2[:, 1, :], ppt[:, 512:1024], rts[:, ts_, :], ALU.mult),
                     reads=R((pres, 1), rres), writes=R(("xs",)), tag=f"{tg}.rb{h}{which}")
                b.op("dve", lambda ppt=ppt, ts_=ts_, rts=rts: nc.vector.tensor_tensor(ct2[:, 0, :], ppt[:, 0:512], rts[:, ts_, :], ALU.mult),
                     reads=R((pres, 0), rres), writes=R(("ct",)), tag=f"{tg}.rc{h}{which}")
                b.op("dve", lambda ppt=ppt, tc=tc, rts=rts: nc.vector.tensor_tensor(ct2[:, 1, :], ppt[:, 512:1024], rts[:, tc, :], ALU.mult),
                     reads=R((pres, 1), rres), writes=R(("ct",)), tag=f"{tg}.rd{h}{which}")
                b.op("dve", lambda dst=dst, h=h: nc.vector.tensor_tensor(dst[:, 2 * h, :], xs2[:, 0, :], xs2[:, 1, :], ALU.subtract),
                     reads=R(("xs",)), writes=R((dres, 2 * h)), tag=f"{tg}.r0{h}{which}")
                b.op("dve", lambda dst=dst, h=h: nc.vector.tensor_tensor(dst[:, 2 * h + 1, :], ct2[:, 0, :], ct2[:, 1, :], ALU.add),
                     reads=R(("ct",)), writes=R((dres, 2 * h + 1)), tag=f"{tg}.r1{h}{which}")
            if mode == "linear":
                for cci in range(2):
                    for t in range(4):
                        b.op("pe", lambda cci=cci, t=t, h=h: nc.tensor.matmul(
                            pp[2][:, cci * 512 + t * P: cci * 512 + (t + 1) * P],
                            kp[:, 2 * h + cci, t * P:(t + 1) * P], id_sb[:], start=True, stop=True),
                            reads=R(("kp", 2 * h + cci), ("id",)), writes=R(("pp2", cci)), tag=f"{tg}.kt{h}{cci}{t}")
                b.op("act", lambda h=h: nc.scalar.activation(
                    ktm_dst(h), pp[2][:].rearrange("p (g t n) -> p g t n", t=4, n=P),
                    AF.Identity, scale=_gch()[h]),
                    reads=R(("pp2", 0), ("pp2", 1)), writes=R(("ktm", h)), tag=f"{tg}.ktE{h}")

    def store_kv_hist(sc):
        b.dma("act", "skst", lambda sc=sc: nc.scalar.dma_start(
            out=scr["k_hist"][sc], in_=kp[:]),
            reads=R(*[("kp", i) for i in range(8)]), writes=R(("d.kh", sc)), tag=f"kst{sc}")

    def store_v_hist(sc):
        b.dma("act", "svst", lambda sc=sc: nc.scalar.dma_start(
            out=scr["v_hist"][sc], in_=v_sb[:]),
            reads=R(*[("v", q, i) for q in range(NQ) for i in range(2)]),
            writes=R(("d.vh", sc)), tag=f"vst{sc}")

    def stage_retention_quad(sc):
        tg = f"Rq.{sc}"
        SCB = [("pp2", 0), ("pp2", 1), ("pp3", 0), ("pp3", 1)]
        SPP = [pp[2], pp[2], pp[3], pp[3]]
        SOF = [0, 512, 0, 512]
        for h in range(H):
            for bi, bb in enumerate(range(sc, -1, -1)):
                dlt = sc - bb
                ping = bi % 2
                if bb < sc:
                    b.dma("syncq", f"skh{ping}", lambda h=h, bb=bb, ping=ping: nc.sync.dma_start(
                        out=khb[:, ping, :, :], in_=scr["k_hist"][bb][:, 2 * h:2 * h + 2, :]),
                        reads=R(("d.kh", bb)), writes=R(("khb", ping)), tag=f"{tg}.lk{h}{bb}")
                    b.dma("act", f"svh{ping}", lambda h=h, bb=bb, ping=ping: nc.scalar.dma_start(
                        out=vhb[:, ping, :, :], in_=scr["v_hist"][bb][:, :, h * 512:(h + 1) * 512]),
                        reads=R(("d.vh", bb)), writes=R(("vhb", ping)), tag=f"{tg}.lv{h}{bb}")
                b.dma("act", f"smk{ping}", lambda h=h, dlt=dlt, ping=ping: nc.scalar.dma_start(
                    out=msk[:, ping, :, :], in_=A["c_qmask"][h, 4 * dlt:4 * dlt + 4].rearrange("m p n -> p m n")),
                    reads=R(), writes=R(("msk", ping)), tag=f"{tg}.lm{h}{dlt}")
                for jb in range(NQ):
                    spp, soff, sbank = SPP[jb], SOF[jb], SCB[jb]
                    for cci in range(2):
                        if bb == sc:
                            lhs = kp[:, 2 * h + cci, jb * P:(jb + 1) * P]
                            lres = ("kp", 2 * h + cci)
                        else:
                            lhs = khb[:, ping, cci, jb * P:(jb + 1) * P]
                            lres = ("khb", ping)
                        b.op("pe", lambda spp=spp, soff=soff, lhs=lhs, cci=cci, h=h: nc.tensor.matmul(
                            spp[:, soff:soff + 512], lhs, qp[:, 2 * h + cci, :],
                            start=(cci == 0), stop=(cci == 1)),
                            reads=R(lres, ("qp", 2 * h + cci)), writes=R(sbank), tag=f"{tg}.s{h}{bb}{jb}{cci}")
                    b.op("dve", lambda spp=spp, soff=soff, jb=jb, ping=ping: nc.vector.tensor_tensor(
                        sct[:, jb, :], spp[:, soff:soff + 512], msk[:, ping, 3 - jb, :], ALU.mult),
                        reads=R(sbank, ("msk", ping)), writes=R(("sct", jb)), tag=f"{tg}.e{h}{bb}{jb}")
                for jb in range(NQ):
                    for qb in range(NQ):
                        ppo = pp[qb // 2]
                        offo = (qb % 2) * 512
                        obank = (PPN[qb // 2], qb % 2)
                        if bb == sc:
                            vsrc = v_sb[:, jb, h * 512:(h + 1) * 512]
                            vres = ("v", jb, h // 2)
                        else:
                            vsrc = vhb[:, ping, jb, :]
                            vres = ("vhb", ping)
                        b.op("pe", lambda ppo=ppo, offo=offo, jb=jb, qb=qb, vsrc=vsrc, st=(bi == 0 and jb == 0), sp=(bi == sc and jb == NQ - 1): nc.tensor.matmul(
                            ppo[:, offo:offo + 512], sct[:, jb, qb * P:(qb + 1) * P], vsrc,
                            start=st, stop=sp),
                            reads=R(("sct", jb), vres), writes=R(obank), tag=f"{tg}.a{h}{bb}{jb}{qb}")
            for qb in range(NQ):
                ppt = pp[qb // 2]
                off = (qb % 2) * 512
                bank = (PPN[qb // 2], qb % 2)
                b.op("act", lambda ppt=ppt, off=off, h=h, qb=qb: nc.scalar.activation(
                    t2[:], ppt[:, off:off + 512], AF.Square, accum_out=acol(h, qb)),
                    reads=R(bank), writes=R(("sta", h, 0), ("t2",)), tag=f"{tg}.sq{h}{qb}")
            b.op("act", lambda h=h: nc.scalar.activation(arow(h, 80), arow(h, 64), AF.Sqrt,
                                                         bias=eps_sb[:], scale=1.0 / 512.0),
                 reads=R(("eps",), ("sta", h, 0)), writes=R(("sta", h, 1)), tag=f"{tg}.rq{h}")
            b.op("dve", lambda h=h: nc.vector.reciprocal(arow(h, 96), arow(h, 80)),
                 reads=R(("sta", h, 1)), writes=R(("sta", h, 2)), tag=f"{tg}.rr{h}")
            for qb in range(NQ):
                ppt = pp[qb // 2]
                off = (qb % 2) * 512
                bank = (PPN[qb // 2], qb % 2)
                b.op("dve", lambda ppt=ppt, off=off, h=h, qb=qb: nc.vector.tensor_scalar(
                    t1[:], ppt[:, off:off + 512], arc(h, qb), None, ALU.mult),
                    reads=R(bank, ("sta", h, 2)), writes=R(("t1",)), tag=f"{tg}.t1{h}{qb}")
                b.op("pool", lambda: nc.gpsimd.tensor_tensor(t2[:], t1[:], gnw_sb[:], ALU.mult),
                     reads=R(("t1",), ("gnw",)), writes=R(("t2",)), tag=f"{tg}.t2{h}{qb}")
                b.op("pool", lambda h=h, qb=qb: nc.gpsimd.tensor_tensor(
                    g_sb[:, qb, h * 512:(h + 1) * 512], t2[:], g_sb[:, qb, h * 512:(h + 1) * 512], ALU.mult),
                    reads=R(("t2",), ("g", qb, h // 2)), writes=R(("g", qb, h // 2)), tag=f"{tg}.og{h}{qb}")

    def stage_vg(sc, wname):
        tg = f"{wname}.{sc}"
        dst = v_sb if wname == "wv" else g_sb
        dres = "v" if wname == "wv" else "g"
        for q in range(NQ):
            for npair in range(2):
                ppt = pp[npair]
                pres = PPN[npair]
                for nh in range(2):
                    n = npair * 2 + nh
                    for j in range(8):
                        s, g0, ncols = wslot(wname, j)
                        b.op("pe", lambda ppt=ppt, nh=nh, n=n, j=j, q=q, s=s, g0=g0, ncols=ncols: nc.tensor.matmul(
                            ppt[:, nh * 512:(nh + 1) * 512], fmT[:, j, q * P:(q + 1) * P],
                            wg_sb[s][:].rearrange("p (c n) -> p c n", n=ncols)[:, j - g0, n * 512:(n + 1) * 512],
                            start=(j == 0), stop=(j == 7)),
                            reads=R(("wg", s), ("fmT", q)), writes=R((pres, nh)), tag=f"{tg}.m{q}{n}{j}")
                fn = AF.Identity if wname == "wv" else AF.Silu
                b.op("act", lambda ppt=ppt, q=q, npair=npair, dst=dst, fn=fn: nc.scalar.activation(
                    dst[:, q, npair * 1024:(npair + 1) * 1024], ppt[:], fn),
                    reads=R((pres, 0), (pres, 1)), writes=R((dres, q, npair)), tag=f"{tg}.e{q}{npair}")

    def stage_retention(sc):
        tg = f"R.{sc}"
        for h in range(H):
            for jb in range(NQ):
                nfree = SC - jb * P
                off = (jb % 2) * 512
                bank = ("pp3", jb % 2)
                for cci in range(2):
                    b.op("pe", lambda off=off, jb=jb, cci=cci, h=h, nfree=nfree: nc.tensor.matmul(
                        pp[3][:, off:off + nfree], kp[:, 2 * h + cci, jb * P:(jb + 1) * P],
                        qp[:, 2 * h + cci, jb * P:SC], start=(cci == 0), stop=(cci == 1)),
                        reads=R(("kp", 2 * h + cci), ("qp", 2 * h + cci)), writes=R(bank), tag=f"{tg}.s{h}{jb}{cci}")
                b.op("dve", lambda off=off, jb=jb, nfree=nfree: nc.vector.tensor_tensor(
                    sct[:, jb, 0:nfree], pp[3][:, off:off + nfree], tri_sb[:, 0:nfree], ALU.mult),
                    reads=R(bank, ("tri",)), writes=R(("sct", jb)), tag=f"{tg}.sE{h}{jb}")
            for qb in range(NQ):
                ppi = qb // 2
                ppt = pp[ppi]
                off = (qb % 2) * 512
                bank = (PPN[ppi], qb % 2)
                for jb in range(qb + 1):
                    b.op("pe", lambda ppt=ppt, off=off, jb=jb, qb=qb, h=h: nc.tensor.matmul(
                        ppt[:, off:off + 512], sct[:, jb, (qb - jb) * P:(qb - jb + 1) * P],
                        v_sb[:, jb, h * 512:(h + 1) * 512], start=(jb == 0), stop=False),
                        reads=R(("sct", jb), ("v", jb, h // 2)), writes=R(bank), tag=f"{tg}.i{h}{qb}{jb}")
                for cci in range(2):
                    b.op("pe", lambda ppt=ppt, off=off, cci=cci, qb=qb, h=h: nc.tensor.matmul(
                        ppt[:, off:off + 512], qp[:, 2 * h + cci, qb * P:(qb + 1) * P],
                        scr_s[:, 2 * h + cci, :], start=False, stop=(cci == 1)),
                        reads=R(("qp", 2 * h + cci), ("scr", h, cci)), writes=R(bank), tag=f"{tg}.x{h}{qb}{cci}")
                b.op("act", lambda ppt=ppt, off=off, h=h, qb=qb: nc.scalar.activation(
                    t2[:], ppt[:, off:off + 512], AF.Square, accum_out=acol(h, qb)),
                    reads=R(bank), writes=R(("sta", h, 0), ("t2",)), tag=f"{tg}.sq{h}{qb}")
            b.op("act", lambda h=h: nc.scalar.activation(arow(h, 80), arow(h, 64), AF.Sqrt,
                                                         bias=eps_sb[:], scale=1.0 / 512.0),
                 reads=R(("eps",), ("sta", h, 0)), writes=R(("sta", h, 1)), tag=f"{tg}.rq{h}")
            b.op("dve", lambda h=h: nc.vector.reciprocal(arow(h, 96), arow(h, 80)),
                 reads=R(("sta", h, 1)), writes=R(("sta", h, 2)), tag=f"{tg}.rr{h}")
            for qb in range(NQ):
                ppi = qb // 2
                ppt = pp[ppi]
                off = (qb % 2) * 512
                bank = (PPN[ppi], qb % 2)
                b.op("dve", lambda ppt=ppt, off=off, h=h, qb=qb: nc.vector.tensor_scalar(
                    t1[:], ppt[:, off:off + 512], arc(h, qb), None, ALU.mult),
                    reads=R(bank, ("sta", h, 2)), writes=R(("t1",)), tag=f"{tg}.t1{h}{qb}")
                b.op("pool", lambda: nc.gpsimd.tensor_tensor(t2[:], t1[:], gnw_sb[:], ALU.mult),
                     reads=R(("t1",), ("gnw",)), writes=R(("t2",)), tag=f"{tg}.t2{h}{qb}")
                b.op("pool", lambda h=h, qb=qb: nc.gpsimd.tensor_tensor(
                    g_sb[:, qb, h * 512:(h + 1) * 512], t2[:], g_sb[:, qb, h * 512:(h + 1) * 512], ALU.mult),
                    reads=R(("t2",), ("g", qb, h // 2)), writes=R(("g", qb, h // 2)), tag=f"{tg}.og{h}{qb}")
            for cci in range(2):
                b.op("pe", lambda cci=cci, h=h: nc.tensor.matmul(
                    pp[2][:, cci * 512:(cci + 1) * 512], id_sb[:], u_s[:, 2 * h + cci, :],
                    start=True, stop=False),
                    reads=R(("id",), ("u", h, cci)), writes=R(("pp2", cci)), tag=f"{tg}.up{h}{cci}")
            for t in range(4):
                for cci in range(2):
                    b.op("pe", lambda t=t, cci=cci, h=h: nc.tensor.matmul(
                        pp[2][:, cci * 512:(cci + 1) * 512], ktm[:, t, (2 * h + cci) * P:(2 * h + cci + 1) * P],
                        v_sb[:, t, h * 512:(h + 1) * 512], start=False, stop=(t == 3)),
                        reads=R(("ktm", h), ("v", t, h // 2)), writes=R(("pp2", cci)), tag=f"{tg}.u{h}{t}{cci}")
            for cci in range(2):
                b.op("dve", lambda cci=cci, h=h: nc.vector.tensor_copy(
                    scr_s[:, 2 * h + cci, :], pp[2][:, cci * 512:(cci + 1) * 512]),
                    reads=R(("pp2", cci)), writes=R(("scr", h, cci)), tag=f"{tg}.cE{h}{cci}")
                b.op("act", lambda cci=cci, h=h: nc.scalar.activation(
                    u_s[:, 2 * h + cci, :], pp[2][:, cci * 512:(cci + 1) * 512], AF.Identity, scale=_gch()[h]),
                    reads=R(("pp2", cci)), writes=R(("u", h, cci)), tag=f"{tg}.uE{h}{cci}")

    def stage_op(sc):
        tg = f"O.{sc}"
        for q in range(NQ):
            for d in range(16):
                ppt = pp[2] if d < 8 else pp[3]
                pres = PPN[2] if d < 8 else PPN[3]
                dd = d % 8
                bankh = dd // 4
                b.op("pe", lambda ppt=ppt, dd=dd, d=d, q=q, bankh=bankh: nc.tensor.matmul(
                    ppt[:, bankh * 512 + (dd % 4) * P: bankh * 512 + (dd % 4 + 1) * P],
                    g_sb[:, q, d * P:(d + 1) * P], id_sb[:], start=True, stop=True),
                    reads=R(("g", q, d // 8), ("id",)), writes=R((pres, bankh)), tag=f"{tg}.t{q}{d}")
            b.op("act", lambda: nc.scalar.activation(ogT[:, 0:8, :], pp[2][:].rearrange("p (j n) -> p j n", n=P), AF.Identity),
                 reads=R(("pp2", 0), ("pp2", 1)), writes=R(("ogT", 0)), tag=f"{tg}.tEa{q}")
            b.op("act", lambda: nc.scalar.activation(ogT[:, 8:16, :], pp[3][:].rearrange("p (j n) -> p j n", n=P), AF.Identity),
                 reads=R(("pp3", 0), ("pp3", 1)), writes=R(("ogT", 1)), tag=f"{tg}.tEb{q}")
            ppi = q % 2
            ppt = pp[ppi]
            pres = PPN[ppi]
            for n in range(2):
                for d in range(16):
                    s, g0, ncols = wslot("wo", d)
                    b.op("pe", lambda ppt=ppt, n=n, d=d, s=s, g0=g0, ncols=ncols: nc.tensor.matmul(
                        ppt[:, n * 512:(n + 1) * 512], ogT[:, d, :],
                        wg_sb[s][:].rearrange("p (c n) -> p c n", n=ncols)[:, d - g0, n * 512:(n + 1) * 512],
                        start=(d == 0), stop=(d == 15)),
                        reads=R(("ogT", d // 8), ("wg", s)), writes=R((pres, n)), tag=f"{tg}.m{q}{n}{d}")
            for n in range(2):
                b.op("dve", lambda ppt=ppt, q=q, n=n: nc.vector.tensor_tensor(
                    x_sb[:, q, n * 512:(n + 1) * 512], ppt[:, n * 512:(n + 1) * 512],
                    x_sb[:, q, n * 512:(n + 1) * 512], ALU.add),
                    reads=R((pres, n), ("x", q)), writes=R(("x", q)), tag=f"{tg}.r{q}{n}")

    bar_res = ([("vstage",), ("gstage",)]
               + [("v", q, npp) for q in range(NQ) for npp in range(2)]
               + [("g", q, npp) for q in range(NQ) for npp in range(2)]
               + [("gstage", k) for k in range(8)])
    b.op("pool", lambda: nc.gpsimd.memset(t1[:, 0:2], 0.0), writes=bar_res, tag="prepbarB")

    def dump_dram(nm, src_ap, reads, tag):
        if not dbg:
            return
        b.dma("pool", "sdbg", lambda nm=nm, src_ap=src_ap: nc.gpsimd.dma_start(out=DBG[nm], in_=src_ap),
              reads=reads, writes=R(("dbg", nm), ("dbgchain",)), tag="dumpd:" + tag)

    def dump(nm, src_ap, reads, tag, tok_major=False):
        if not dbg:
            return
        dst = DBG[nm].rearrange("(q p) d -> p q d", p=P) if tok_major else DBG[nm]
        b.dma("pool", "sdbg", lambda dst=dst, src_ap=src_ap: nc.gpsimd.dma_start(out=dst, in_=src_ap),
              reads=reads, writes=R(("dbg", nm), ("dbgchain",)), tag="dump:" + tag)

    for q in range(NQ):
        emit_x_load(0, q)
    for sc in range(NSC):
        if sc > 0:
            emit_granule_loads(sc, ("wq", "wk"))
            emit_rope_load(sc, 0)
        if sc == 0:
            dump_dram("d_wq", scr["wq_wt"], R(*[("d.wq.wt", i) for i in range(NWT["wq"])]), "wq")
            dump_dram("d_wv", scr["wv_wt"], R(*[("d.wv.wt", i) for i in range(NWT["wv"])]), "wv")
        stage_ffn(sc, 1)
        if sc == 0:
            dump("d_x1", x_sb[:], R(*[("x", q) for q in range(NQ)]), "x1", tok_major=True)
            dump("d_tern", tern[:], R(*[("tern", oc) for oc in range(8)]), "tern")
        stage_ln2(sc)
        if sc == 0:
            dump("d_fmT", fmT[:], R(*[("fmT", q) for q in range(NQ)]), "fmT")
        stage_qk(sc)
        if sc == 0:
            dump("d_qp", qp[:], R(*[("qp", i) for i in range(8)]), "qp")
            dump("d_kp", kp[:], R(*[("kp", i) for i in range(8)]), "kp")
            if mode == "linear":
                dump("d_ktm", ktm[:], R(*[("ktm", h) for h in range(H)]), "ktm")
        if mode == "quad":
            store_kv_hist(sc)
        emit_granule_loads(sc, ("wv",))
        stage_vg(sc, "wv")
        if mode == "quad":
            store_v_hist(sc)
        emit_granule_loads(sc, ("wg",))
        stage_vg(sc, "wg")
        if sc == 0:
            dump("d_v", v_sb[:], R(*[("v", q, i) for q in range(NQ) for i in range(2)]), "v")
            dump("d_g", g_sb[:], R(*[("g", q, i) for q in range(NQ) for i in range(2)]), "g")
        emit_granule_loads(sc, ("wo",))
        if mode == "linear":
            stage_retention(sc)
        else:
            stage_retention_quad(sc)
        if sc == 0:
            dump("d_og", g_sb[:], R(*[("g", q, i) for q in range(NQ) for i in range(2)]), "og")
            if mode == "linear":
                dump("d_s", scr_s[:], R(*[("scr", h, c) for h in range(H) for c in range(2)]), "s")
            dump("d_sct", sct[:], R(*[("sct", jb) for jb in range(NQ)]), "sct")
        stage_op(sc)
        if sc == 0:
            dump("d_x2", x_sb[:], R(*[("x", q) for q in range(NQ)]), "x2", tok_major=True)
        stage_ffn(sc, 2)
    b.op("syncq", lambda: nc.sync.nop(),
         reads=R(*[("o", sc, q) for sc in range(NSC) for q in range(NQ)]), tag="fin")

    b.finalize()

    with nc.Block() as block:
        @block.tensor
        def _(eng):
            b.emit(eng, "pe", sems)

        @block.vector
        def _(eng):
            b.emit(eng, "dve", sems)

        @block.scalar
        def _(eng):
            b.emit(eng, "act", sems)

        @block.gpsimd
        def _(eng):
            b.emit(eng, "pool", sems)

        @block.sync
        def _(eng):
            b.emit(eng, "syncq", sems)

    ctx.close()
    return nc


def kernel(**inputs):
    tabs = _host_tables()
    nc = build_program()
    x = np.ascontiguousarray(inputs["L0_attn_x"])
    base = {k: np.ascontiguousarray(v) for k, v in inputs.items() if k != "L0_attn_x"}
    base.update(tabs)
    in_maps = [dict(base, L0_attn_x=x[core]) for core in range(8)]
    res = run_bass_kernel_spmd(nc, in_maps, core_ids=list(range(8)))
    return np.stack([np.asarray(r["out"]) for r in res.results], axis=0)


# version 24
# speedup vs baseline: 1.2034x; 1.0095x over previous
import numpy as np
import ml_dtypes
import contextlib
import concourse.bass as bass
import concourse.mybir as mybir
from concourse.bass_utils import run_bass_kernel_spmd

F32 = mybir.dt.float32
BF16 = mybir.dt.bfloat16
I32 = mybir.dt.int32
AF = mybir.ActivationFunctionType
ALU = mybir.AluOpType
AX = mybir.AxisListType

T, D = 4096, 1024
H = 4
SC, NSC, NQ = 512, 8, 4
P = 128
EPS = 1e-5
GAMMA = [1.0 - 2.0 ** (-5 - h) for h in range(H)]


def _log_decay_f32():
    import jax
    import jax.numpy as jnp
    with jax.default_device(jax.devices("cpu")[0]):
        return np.asarray(jnp.log1p(-jnp.exp2(-5.0 - jnp.arange(H, dtype=jnp.float32))))


_LD = None


def _ld():
    global _LD
    if _LD is None:
        _LD = _log_decay_f32().astype(np.float64)
    return _LD


def _gch():
    return [float(np.float32(np.exp(_ld()[h] * SC))) for h in range(H)]
NWG = 5
ATTN_MODE = "linear"
ENGINES = ("pe", "dve", "act", "pool", "syncq")
_ENG_NC = {"syncq": "sync", "act": "scalar", "pool": "gpsimd", "dve": "vector", "pe": "tensor"}


class _Op:
    __slots__ = ("eng", "fn", "deps", "kept", "seq", "inc", "sem", "semval",
                 "clock_before", "tag", "is_dma")

    def __init__(self, eng, fn, seq, tag, is_dma, sem):
        self.eng, self.fn, self.seq, self.tag = eng, fn, seq, tag
        self.is_dma, self.sem = is_dma, sem
        self.deps, self.kept = [], []
        self.inc = False
        self.semval = None
        self.clock_before = None


class PB:

    def __init__(self):
        self.ops = {e: [] for e in ENGINES}
        self.order = []
        self.last_w = {}
        self.readers = {}
        self.dma_count = {}

    def op(self, eng, fn, reads=(), writes=(), tag="", dma_sem=None):
        o = _Op(eng, fn, len(self.ops[eng]), tag, dma_sem is not None, dma_sem)
        if dma_sem is not None:
            self.dma_count[dma_sem] = self.dma_count.get(dma_sem, 0) + 1
            o.semval = 16 * self.dma_count[dma_sem]
        deps = []
        for r in reads:
            w = self.last_w.get(r)
            if w is not None:
                deps.append(w)
        for r in writes:
            w = self.last_w.get(r)
            if w is not None:
                deps.append(w)
            deps.extend(self.readers.get(r, ()))
        seen = set()
        for d in deps:
            if d is o or id(d) in seen:
                continue
            seen.add(id(d))
            if d.eng == "pe" and eng == "pe" and not d.is_dma and not o.is_dma:
                continue
            o.deps.append(d)
        for r in reads:
            self.readers.setdefault(r, []).append(o)
        for r in writes:
            self.last_w[r] = o
            self.readers[r] = []
        self.ops[eng].append(o)
        self.order.append(o)
        return o

    def dma(self, eng, sem, fn, reads=(), writes=(), tag=""):
        return self.op(eng, fn, reads, writes, tag, dma_sem=sem)

    @staticmethod
    def _merge(a, b):
        for k, v in b.items():
            if a.get(k, -1) < v:
                a[k] = v

    def _cclock(self, d):
        c = dict(d.clock_before)
        c[d.sem if d.is_dma else d.eng] = d.semval if d.is_dma else d.seq
        return c

    def finalize(self):
        for o in self.order:
            prev = self.ops[o.eng][o.seq - 1] if o.seq > 0 else None
            clock = {} if prev is None else dict(prev.clock_before)
            if prev is not None:
                for d in prev.kept:
                    self._merge(clock, self._cclock(d))
            o.clock_before = clock
            for d in o.deps:
                dom = d.sem if d.is_dma else d.eng
                val = d.semval if d.is_dma else d.seq
                if clock.get(dom, -1) >= val:
                    continue
                o.kept.append(d)
                self._merge(clock, self._cclock(d))
                if not d.is_dma:
                    d.inc = True
        for e in ENGINES:
            c = 0
            for o in self.ops[e]:
                if o.is_dma:
                    o.inc = True
                elif o.inc:
                    c += 1
                    o.semval = c

    def emit(self, eng_obj, eng_name, sems):
        own = sems[eng_name]
        for o in self.ops[eng_name]:
            waits = {}
            for d in o.kept:
                key = d.sem if d.is_dma else d.eng
                val = d.semval if d.is_dma else (d.semval if d.inc else None)
                if val is None:
                    raise RuntimeError(f"{o.tag}: dep {d.tag} has no semval")
                if waits.get(key, -1) < val:
                    waits[key] = val
            for key in sorted(waits):
                eng_obj.wait_ge(sems[key], waits[key])
            instr = o.fn()
            if o.is_dma:
                instr.then_inc(sems[o.sem], 16)
            elif o.inc:
                instr.then_inc(own, 1)


_TBL_CACHE = {}


def _host_tables(mode=None):
    mode = ATTN_MODE if mode is None else mode
    if _TBL_CACHE:
        return dict(_TBL_CACHE)
    import jax
    import jax.numpy as jnp
    with jax.default_device(jax.devices("cpu")[0]):
        inv_freq = 1.0 / 10000.0 ** (jnp.arange(0, 256, 2, dtype=jnp.float32) / 256)
        freqs = jnp.arange(T, dtype=jnp.float32)[:, None] * inv_freq[None, :]
        cos = np.asarray(jnp.cos(freqs)).astype(np.float64).T
        sin = np.asarray(jnp.sin(freqs)).astype(np.float64).T
    ld = _ld()
    jj = np.arange(P)[:, None]
    cc_ = np.arange(SC)[None, :]
    tri = ((cc_ >= jj) | (cc_ >= P)).astype(np.float32)
    if mode == "linear":
        i_in = (np.arange(T) % SC).astype(np.float64)
        rope = np.zeros((H, 4, 128, T), dtype=np.float64)
        for h in range(H):
            gq = np.exp(ld[h] * i_in) / 16.0
            gk = np.exp(-ld[h] * i_in)
            rope[h, 0] = cos * gq[None, :]
            rope[h, 1] = sin * gq[None, :]
            rope[h, 2] = cos * gk[None, :]
            rope[h, 3] = sin * gk[None, :]
        _TBL_CACHE.update(dict(c_rope=np.ascontiguousarray(rope.astype(np.float32))))
    else:
        rope = np.stack([cos / 16.0, sin / 16.0, cos, sin]).astype(np.float32)
        _TBL_CACHE.update(dict(c_rope=np.ascontiguousarray(rope)))
        import jax
        import jax.numpy as jnp
        ldf32 = _log_decay_f32()
        with jax.default_device(jax.devices("cpu")[0]):
            dr = jnp.arange(0, 4096, dtype=jnp.float32)
            m1 = np.stack([np.asarray(jnp.exp(ldf32[h] * dr)) for h in range(H)])
        qmask = np.zeros((H, 32, P, SC), dtype=np.float32)
        dd = (np.arange(SC)[None, :] - np.arange(P)[:, None])
        for h in range(H):
            for k0 in range(32):
                d = (k0 - 3) * P + dd
                valid = d >= 0
                qmask[h, k0][valid] = m1[h][d[valid]]
        _TBL_CACHE.update(dict(c_qmask=np.ascontiguousarray(qmask)))
    _TBL_CACHE.update(dict(
        c_tri=np.ascontiguousarray(tri),
        c_id=np.eye(P, dtype=ml_dtypes.bfloat16),
        c_ones=np.ones((1, P), dtype=np.float32),
    ))
    return dict(_TBL_CACHE)


def build_program(dbg=False, mode=None):
    mode = ATTN_MODE if mode is None else mode
    global NWG
    NWG = 4 if mode == "linear" else 5
    nc = bass.Bass("TRN2", target_bir_lowering=False)
    dt_in = {
        "L0_attn_x": ([T, D], F32), "L0_attn_Wq": ([D, D], F32),
        "L0_attn_Wk": ([D, D], F32), "L0_attn_Wv": ([2048, D], F32),
        "L0_attn_Wg": ([2048, D], F32), "L0_attn_Wo": ([D, 2048], F32),
        "L0_attn_g_norm_w": ([512], F32), "L0_ffn_packed_weight": ([262144], I32),
        "L0_ffn_weight_scale": ([1], F32), "L0_ffn_bias": ([D], F32),
        "c_rope": ([H, 4, 128, T] if mode == "linear" else [4, 128, T], F32),
        "c_tri": ([P, SC], F32),
        "c_id": ([P, P], BF16), "c_ones": ([1, P], F32),
    }
    if mode == "quad":
        dt_in["c_qmask"] = ([H, 32, P, SC], F32)
    A = {k: nc.dram_tensor(k, sh, dt, kind="ExternalInput").ap() for k, (sh, dt) in dt_in.items()}
    out = nc.dram_tensor("out", [T, D], F32, kind="ExternalOutput").ap()
    DBG = {}
    if dbg:
        for nm, sh, dt in [("d_x1", [SC, D], F32), ("d_x2", [SC, D], F32),
                           ("d_qp", [P, 8, SC], F32), ("d_kp", [P, 8, SC], F32),
                           ("d_ktm", [P, NQ, D], F32), ("d_v", [P, NQ, 2048], F32),
                           ("d_g", [P, NQ, 2048], F32), ("d_og", [P, NQ, 2048], F32),
                           ("d_s", [P, 8, 512], F32), ("d_sct", [P, NQ, SC], F32),
                           ("d_fmT", [P, 8, SC], F32), ("d_tern", [P, 8, D], F32),
                           ("d_q8", [P, 8, SC], F32), ("d_wq", [D, D], F32),
                           ("d_wv", [D, 2048], F32), ("d_rt", [P, 4, SC], F32),
                           ("d_wg0", [P, 4096], F32)]:
            DBG[nm] = nc.dram_tensor(nm, sh, dt, kind="ExternalOutput").ap()
    scr = {}
    for nm, sh in [("wq_bf", [D, D]), ("wk_bf", [D, D]), ("wv_bf", [2048, D]),
                   ("wg_bf", [2048, D]), ("wo_bf", [D, 2048]),
                   ("wq_wt", [D, D]), ("wk_wt", [D, D]), ("wv_wt", [D, 2048]),
                   ("wg_wt", [D, 2048]), ("wo_wt", [2048, D])]:
        scr[nm] = nc.dram_tensor(nm, sh, BF16, kind="Internal").ap()
    if mode == "quad":
        scr["k_hist"] = nc.dram_tensor("k_hist", [NSC, P, 8, 512], BF16, kind="Internal").ap()
        scr["v_hist"] = nc.dram_tensor("v_hist", [NSC, P, NQ, 2048], BF16, kind="Internal").ap()

    ctx = contextlib.ExitStack()
    sb = lambda nm, shape, dt: ctx.enter_context(nc.sbuf_tensor(nm, shape, dt))
    pst = lambda nm, shape: ctx.enter_context(nc.psum_tensor(nm, shape, F32))
    mksem = lambda nm: ctx.enter_context(nc.semaphore(nm))

    x_sb = sb("x_sb", [P, NQ, D], F32)
    fmT = sb("fmT", [P, 8, SC], BF16)
    qp = sb("qp", [P, 8, SC], BF16)
    kp = sb("kp", [P, 8, SC], BF16)
    ktm = sb("ktm", [P, NQ, D], BF16) if mode == "linear" else None
    v_sb = sb("v_sb", [P, NQ, 2048], BF16)
    g_sb = sb("g_sb", [P, NQ, 2048], BF16)
    ogT = sb("ogT", [P, 16, P], BF16)
    sct = sb("sct", [P, NQ, SC], BF16)
    if mode == "linear":
        scr_s = sb("scr_s", [P, 8, 512], BF16)
        u_s = sb("u_s", [P, 8, 512], BF16)
    else:
        scr_s = u_s = None
        msk = sb("msk", [P, 2, 4, SC], F32)
        khb = sb("khb", [P, 2, 2, SC], BF16)
        vhb = sb("vhb", [P, 2, NQ, SC], BF16)
    wg_sb = [sb(f"wgs{i}", [P, 4096], BF16) for i in range(NWG)]
    tern = sb("tern", [P, 8, D], BF16)
    rt_sb = ([sb("rts0", [P, 4, SC], F32), sb("rts1", [P, 4, SC], F32)]
             if mode == "linear" else [sb("rts0", [P, 4, SC], F32)])
    tri_sb = sb("tri_sb", [P, SC], F32)
    id_sb = sb("id_sb", [P, P], BF16)
    ones_sb = sb("ones_sb", [1, P], F32)
    bias_sb = sb("bias_sb", [P, D], F32)
    gnw_sb = sb("gnw_sb", [P, 512], F32)
    ws_sb = sb("ws_sb", [P, 1], F32)
    unp_i32 = sb("unp_i32", [P, D], I32)
    stats = sb("stats", [P, 112], F32)
    xs = sb("xs", [P, D], F32)
    y1 = sb("y1", [P, D], I32)
    ct = sb("ct", [P, D], F32)
    stage_bf = sb("stage_bf", [P, D], BF16)
    t1 = sb("t1", [P, 512], F32)
    t2 = sb("t2", [P, 512], F32)
    tt_f = sb("tt_f", [P, D], F32)
    eps_sb = sb("eps_sb", [P, 1], F32)

    pp = [pst(f"pp{i}", [P, 1024]) for i in range(4)]
    PPN = ["pp0", "pp1", "pp2", "pp3"]

    sems = {e: mksem("sm_" + e) for e in ENGINES}
    for nm in (["sx%d" % q for q in range(NQ)] + ["so%d" % q for q in range(NQ)]
               + ["swg%d" % i for i in range(NWG)] + ["srt0", "srt1", "scst", "pc0", "pc1",
                  "ps0", "ps1", "ps2", "ps3", "sid", "sones", "srow", "spk0", "spk1", "sdbg",
                  "skh0", "skh1", "svh0", "svh1", "smk0", "smk1", "skst", "svst"]):
        sems[nm] = mksem("sm_" + nm)

    b = PB()
    R = lambda *a: list(a)

    def ld(eng, sem, dst_ap, src_ap, writes, reads=(), tag="ld"):
        nceng = getattr(nc, _ENG_NC[eng])
        b.dma(eng, sem, lambda d=dst_ap, s=src_ap, e=nceng: e.dma_start(out=d, in_=s),
              reads=reads, writes=writes, tag=tag)

    CM = dict(amax=0, clamp=1, inv=2, alpha=3, mb=4, sq=8, rstd=9, mbneg=10,
              rs=11, mean=12, var=13, mx=14, mn=15)

    def col(q, name):
        c = q * 16 + CM[name]
        return stats[:, c:c + 1]

    def acol(h, qb):
        return stats[:, 64 + 4 * h + qb: 65 + 4 * h + qb]

    def arow(h, base):
        return stats[:, base + 4 * h: base + 4 * h + 4]

    def arc(h, qb):
        return stats[:, 96 + 4 * h + qb: 97 + 4 * h + qb]

    ld("syncq", "scst", tri_sb[:], A["c_tri"], R(("tri",)))
    ld("syncq", "sid", id_sb[:], A["c_id"], R(("id",)))
    ld("syncq", "sones", ones_sb[:], A["c_ones"], R(("ones",)))
    ld("syncq", "srow", bias_sb[:1, 0:D], A["L0_ffn_bias"][None, :], R(("rowb",)))
    for n in range(2):
        b.op("pe", lambda n=n: nc.tensor.matmul(pp[0][:, n * 512:(n + 1) * 512], ones_sb[:],
                                                bias_sb[:1, n * 512:(n + 1) * 512], start=True, stop=True),
             reads=R(("rowb",), ("ones",)), writes=R(("pp0", n)), tag="pbias%d" % n)
    b.op("act", lambda: nc.scalar.activation(bias_sb[:], pp[0][:], AF.Identity, scale=0.5),
         reads=R(("pp0", 0), ("pp0", 1)), writes=R(("bias",), ("rowb",)), tag="pbiasE")
    ld("syncq", "srow", gnw_sb[:1, 0:512], A["L0_attn_g_norm_w"][None, :], R(("rowg",)), reads=R(("rowb",)))
    b.op("pe", lambda: nc.tensor.matmul(pp[0][:, 0:512], ones_sb[:], gnw_sb[:1, 0:512], start=True, stop=True),
         reads=R(("rowg",), ("ones",), ("bias",)), writes=R(("pp0", 0)), tag="pgnw")
    b.op("act", lambda: nc.scalar.activation(gnw_sb[:], pp[0][:, 0:512], AF.Identity),
         reads=R(("pp0", 0),), writes=R(("gnw",), ("rowg",)), tag="pgnwE")
    ld("syncq", "srow", ws_sb[:1, 0:1], A["L0_ffn_weight_scale"][None, :], R(("roww",)), reads=R(("rowg",)))
    b.op("pe", lambda: nc.tensor.matmul(pp[0][:, 0:1], ones_sb[:], ws_sb[:1, 0:1], start=True, stop=True),
         reads=R(("roww",), ("ones",), ("gnw",)), writes=R(("pp0", 0)), tag="pws")
    b.op("act", lambda: nc.scalar.activation(ws_sb[:], pp[0][:, 0:1], AF.Identity),
         reads=R(("pp0", 0),), writes=R(("ws",), ("roww",)), tag="pwsE")
    b.op("pool", lambda: nc.gpsimd.memset(eps_sb[:], EPS), writes=R(("eps",)), tag="zE")
    if dbg:
        b.op("pool", lambda: nc.gpsimd.memset(sct[:], 0.0), writes=R(*[("sct", jb) for jb in range(NQ)]), tag="zsct")
    if mode == "linear":
        b.op("pool", lambda: nc.gpsimd.memset(scr_s[:], 0.0),
             writes=R(*[("scr", h, c) for h in range(H) for c in range(2)]), tag="zS")
        b.op("pool", lambda: nc.gpsimd.memset(u_s[:], 0.0),
             writes=R(*[("u", h, c) for h in range(H) for c in range(2)]), tag="zU")

    pk = A["L0_ffn_packed_weight"].rearrange("(c p n) -> c p n", p=P, n=D)
    ld("pool", "spk0", y1[:], pk[0], R(("pky",)), tag="pkl0")
    first_chunk1 = [True]
    for pc in (0, 1):
        if pc == 1:
            ld("pool", "spk0", y1[:], pk[1], R(("pky",)), tag="pkl1")
        for oc in range(pc, 8, 2):
            sh = 2 * (oc // 2)
            b.op("dve", lambda sh=sh: nc.vector.tensor_scalar(
                unp_i32[:], y1[:], sh, 3, ALU.logical_shift_right, ALU.bitwise_and),
                reads=R(("pky",),), writes=R(("unp",)), tag=f"unp{oc}")
            b.op("dve", lambda: nc.vector.tensor_scalar(stage_bf[:], unp_i32[:], -1.0, None, ALU.add),
                 reads=R(("unp",),), writes=R(("stage",)), tag=f"unpb{oc}")
            for j in range(8):
                bank = j // 4
                b.op("pe", lambda j=j, bank=bank: nc.tensor.matmul(
                    pp[2][:, bank * 512 + (j % 4) * P: bank * 512 + (j % 4 + 1) * P],
                    stage_bf[:, j * P:(j + 1) * P], id_sb[:], start=True, stop=True),
                    reads=R(("stage",), ("id",)), writes=R(("pp2", bank)), tag=f"tT{oc}{j}")
            b.op("act", lambda oc=oc: nc.scalar.activation(
                tern[:, 0:8, oc * P:(oc + 1) * P],
                pp[2][:].rearrange("p (j n) -> p j n", n=P), AF.Identity),
                reads=R(("pp2", 0), ("pp2", 1)), writes=R(("tern", oc)), tag=f"tE{oc}")

    b.op("pool", lambda: nc.gpsimd.memset(t2[:, 0:2], 0.0),
         writes=[("pky",), ("unp",), ("y1",), ("ct",), ("xs",), ("stage",),
                 ("ttf", 0), ("ttf", 1)], tag="prepbarA")

    v_all = v_sb[:].rearrange("p a (s n) -> p (a s) n", n=D)
    v_all2 = v_sb[:]

    def prep_weight(wname, src, bf, wt, rows, cols, pci, psi):
        nch = rows // P
        seg = src.rearrange("(c p) i -> c p i", p=P)
        bfr = bf.rearrange("(c p) i -> c p i", p=P)
        per_cast = 8 if cols == D else 4
        nh = nch // per_cast
        vv = v_all if cols == D else v_all2
        for hh in range(nh):
            b.dma("pool", f"pc{pci}", lambda hh=hh, seg=seg, vv=vv, per_cast=per_cast: nc.gpsimd.dma_start(
                out=vv[:, 0:per_cast, :],
                in_=seg[hh * per_cast:(hh + 1) * per_cast].rearrange("c p i -> p c i")),
                reads=R(), writes=R(("vstage",)), tag=f"cast:{wname}{hh}")
            b.dma("syncq", f"ps{psi}", lambda hh=hh, bfr=bfr, vv=vv, per_cast=per_cast: nc.sync.dma_start(
                out=bfr[hh * per_cast:(hh + 1) * per_cast].rearrange("c p i -> p c i"),
                in_=vv[:, 0:per_cast, :]),
                reads=R(("vstage",)), writes=R((f"d.{wname}.bf", hh)), tag=f"stbf:{wname}{hh}")
        ncc = cols // P
        per_store = 8192 // rows
        g_view = g_sb[:] if rows == 2048 else g_sb[:].rearrange("p a (s n) -> p (a s) n", n=rows)
        wtr = wt.rearrange("(c p) o -> c p o", p=P)
        done, st_i = 0, 0
        while done < ncc:
            cnt = min(per_store, ncc - done)
            for k in range(cnt):
                c = done + k
                b.dma("syncq", f"ps{psi}", lambda c=c, k=k, bf=bf, rows=rows, g_view=g_view: nc.sync.dma_start(
                    out=g_view[:, k, :], in_=bf[0:rows, c * P:(c + 1) * P], transpose=True),
                    reads=R(*[(f"d.{wname}.bf", hh) for hh in range(nh)], ("gstage",)),
                    writes=R(("gstage", k)), tag=f"tld:{wname}{c}")
            b.dma("syncq", f"ps{(psi + 2) % 4}", lambda done=done, cnt=cnt, wtr=wtr, g_view=g_view: nc.sync.dma_start(
                out=wtr[done:done + cnt].rearrange("c p o -> p c o"), in_=g_view[:, 0:cnt, :]),
                reads=R(*[("gstage", k) for k in range(cnt)]),
                writes=R(("gstage",), (f"d.{wname}.wt", st_i)), tag=f"stwt:{wname}{st_i}")
            done += cnt
            st_i += 1
        return st_i

    GR_PLAN = []
    slot = 0
    for wname, wt, nin_tot, ncols in (("wq", scr["wq_wt"], 8, D), ("wk", scr["wk_wt"], 8, D),
                                      ("wv", scr["wv_wt"], 8, 2048), ("wg", scr["wg_wt"], 8, 2048),
                                      ("wo", scr["wo_wt"], 16, D)):
        per = 4096 // ncols
        for g0 in range(0, nin_tot, per):
            GR_PLAN.append((slot % NWG, wname, wt, g0, per, ncols))
            slot += 1

    def wslot(wname, j):
        for (s, wn, wt, g0, per, ncols) in GR_PLAN:
            if wn == wname and g0 <= j < g0 + per:
                return s, g0, ncols
        raise KeyError((wname, j))

    def emit_granule_loads(sc, which):
        for gi, (s, wname, wt, g0, per, ncols) in enumerate(GR_PLAN):
            if wname not in which:
                continue
            wtr = wt.rearrange("(c p) o -> c p o", p=P)
            b.dma("syncq", f"swg{s}", lambda s=s, wtr=wtr, g0=g0, per=per, ncols=ncols: nc.sync.dma_start(
                out=wg_sb[s][:].rearrange("p (c n) -> p c n", n=ncols)[:, 0:per, :],
                in_=wtr[g0:g0 + per].rearrange("c p o -> p c o")),
                reads=R(*[(f"d.{wname}.wt", i) for i in range(NWT[wname])]),
                writes=R(("wg", s)), tag=f"gld{sc}:{wname}{g0}")
            if dbg and sc == 0 and wname == "wq" and g0 == 0:
                dump("d_wg0", wg_sb[0][:], R(("wg", 0)), "wg0")

    def emit_rope_load(sc, h):
        if mode == "quad":
            if h != 0:
                return
            b.dma("syncq", "srt0", lambda sc=sc: nc.sync.dma_start(
                out=rt_sb[0][:],
                in_=A["c_rope"][:, :, sc * SC:(sc + 1) * SC].rearrange("t p n -> p t n")),
                reads=R(), writes=R(("rt", 0)), tag=f"rtld{sc}")
            return
        b.dma("syncq", f"srt{h % 2}", lambda sc=sc, h=h: nc.sync.dma_start(
            out=rt_sb[h % 2][:],
            in_=A["c_rope"][h, :, :, sc * SC:(sc + 1) * SC].rearrange("t p n -> p t n")),
            reads=R(), writes=R(("rt", h % 2)), tag=f"rtld{sc}.{h}")
        if dbg and sc == 0 and h == 0:
            dump("d_rt", rt_sb[0][:], R(("rt", 0)), "rt0")

    def emit_x_load(sc, q):
        b.dma("act", f"sx{q}", lambda sc=sc, q=q: nc.scalar.dma_start(
            out=x_sb[:, q, :], in_=A["L0_attn_x"][sc * SC + q * P: sc * SC + (q + 1) * P, :]),
            reads=R(), writes=R(("x", q)), tag=f"xld{sc}.{q}")

    NWT = {}
    NWT["wq"] = prep_weight("wq", A["L0_attn_Wq"], scr["wq_bf"], scr["wq_wt"], D, D, 0, 0)
    NWT["wk"] = prep_weight("wk", A["L0_attn_Wk"], scr["wk_bf"], scr["wk_wt"], D, D, 1, 1)
    emit_granule_loads(0, ("wq", "wk"))
    emit_rope_load(0, 0)
    NWT["wv"] = prep_weight("wv", A["L0_attn_Wv"], scr["wv_bf"], scr["wv_wt"], 2048, D, 0, 0)
    NWT["wg"] = prep_weight("wg", A["L0_attn_Wg"], scr["wg_bf"], scr["wg_wt"], 2048, D, 1, 1)
    NWT["wo"] = prep_weight("wo", A["L0_attn_Wo"], scr["wo_bf"], scr["wo_wt"], D, 2048, 0, 0)

    def stats_ln(tg, q, want_minmax):
        b.op("act", lambda q=q: nc.scalar.activation(y1[:], x_sb[:, q, :], AF.Identity, accum_out=col(q, "sq")),
             reads=R(("x", q)), writes=R(("y1",), ("stc", q, 1)), tag=f"{tg}.sum{q}")
        b.op("act", lambda q=q: nc.scalar.activation(y1[:], x_sb[:, q, :], AF.Square, accum_out=col(q, "var")),
             reads=R(("x", q)), writes=R(("y1",), ("stc", q, 2)), tag=f"{tg}.ssq{q}")
        if want_minmax:
            b.op("dve", lambda q=q: nc.vector.tensor_reduce(stats[:, q * 16 + 14:q * 16 + 15], x_sb[:, q, :], AX.X, ALU.max),
                 reads=R(("x", q)), writes=R(("stc", q, 3)), tag=f"{tg}.mx{q}")
            b.op("dve", lambda q=q: nc.vector.tensor_reduce(stats[:, q * 16 + 15:q * 16 + 16], x_sb[:, q, :], AX.X, ALU.min),
                 reads=R(("x", q)), writes=R(("stc", q, 3)), tag=f"{tg}.mn{q}")

    def stats_fin(tg, q):
        b.op("dve", lambda q=q: nc.vector.tensor_scalar(col(q, "mean"), col(q, "sq"), 1.0 / D, None, ALU.mult),
             reads=R(("stc", q, 1)), writes=R(("stc", q, 2)), tag=f"{tg}.mm_{q}")
        b.op("dve", lambda q=q: nc.vector.scalar_tensor_tensor(col(q, "sq"), col(q, "mean"), 1.0, col(q, "mean"), ALU.mult, ALU.mult),
             reads=R(("stc", q, 2)), writes=R(("stc", q, 1)), tag=f"{tg}.m2{q}")
        b.op("dve", lambda q=q: nc.vector.scalar_tensor_tensor(col(q, "var"), col(q, "var"), 1.0 / D, col(q, "sq"), ALU.mult, ALU.subtract),
             reads=R(("stc", q, 1), ("stc", q, 2)), writes=R(("stc", q, 2)), tag=f"{tg}.var{q}")

    def batched_rstd(tg):
        b.op("act", lambda: nc.scalar.activation(stats[:, 8:64:16], stats[:, 13:64:16], AF.Sqrt, bias=eps_sb[:]),
             reads=R(("eps",), *[("stc", q, 2) for q in range(NQ)]),
             writes=R(*[("stc", q, 1) for q in range(NQ)]), tag=f"{tg}.sqv")
        b.op("dve", lambda: nc.vector.reciprocal(stats[:, 9:64:16], stats[:, 8:64:16]),
             reads=R(*[("stc", q, 1) for q in range(NQ)]),
             writes=R(*[("stc", q, 1) for q in range(NQ)]), tag=f"{tg}.rst")

    def quant_scalars(tg, q):
        b.op("dve", lambda q=q: nc.vector.tensor_tensor(col(q, "amax"), col(q, "mx"), col(q, "mean"), ALU.subtract),
             reads=R(("stc", q, 2)), writes=R(("stc", q, 0)), tag=f"{tg}.qa{q}")
        b.op("dve", lambda q=q: nc.vector.tensor_tensor(col(q, "mb"), col(q, "mean"), col(q, "mn"), ALU.subtract),
             reads=R(("stc", q, 2)), writes=R(("stc", q, 0)), tag=f"{tg}.qb{q}")
        b.op("dve", lambda q=q: nc.vector.tensor_tensor(col(q, "amax"), col(q, "amax"), col(q, "mb"), ALU.max),
             reads=R(("stc", q, 0)), writes=R(("stc", q, 0)), tag=f"{tg}.qm{q}")
        b.op("dve", lambda q=q: nc.vector.scalar_tensor_tensor(col(q, "clamp"), col(q, "amax"), 1.0, col(q, "rstd"), ALU.mult, ALU.mult),
             reads=R(("stc", q, 0), ("stc", q, 1)), writes=R(("stc", q, 0)), tag=f"{tg}.qs{q}")
        b.op("dve", lambda q=q: nc.vector.tensor_scalar(col(q, "clamp"), col(q, "clamp"), 1e-5, None, ALU.max),
             reads=R(("stc", q, 0)), writes=R(("stc", q, 0)), tag=f"{tg}.qc{q}")
        b.op("dve", lambda q=q: nc.vector.reciprocal(col(q, "inv"), col(q, "clamp")),
             reads=R(("stc", q, 0)), writes=R(("stc", q, 0)), tag=f"{tg}.qi{q}")
        b.op("dve", lambda q=q: nc.vector.scalar_tensor_tensor(col(q, "rs"), col(q, "inv"), 127.0, col(q, "rstd"), ALU.mult, ALU.mult),
             reads=R(("stc", q, 0), ("stc", q, 1)), writes=R(("stc", q, 1)), tag=f"{tg}.qr{q}")
        b.op("dve", lambda q=q: nc.vector.scalar_tensor_tensor(col(q, "mbneg"), col(q, "mean"), -1.0, col(q, "rs"), ALU.mult, ALU.mult),
             reads=R(("stc", q, 2), ("stc", q, 1)), writes=R(("stc", q, 1)), tag=f"{tg}.qn{q}")
        b.op("dve", lambda q=q: nc.vector.scalar_tensor_tensor(col(q, "alpha"), col(q, "clamp"), 0.5 / 127.0, ws_sb[:], ALU.mult, ALU.mult),
             reads=R(("stc", q, 0), ("ws",)), writes=R(("stc", q, 0)), tag=f"{tg}.ql{q}")

    def transposes_to_fmT(tg, q, src_res, ppi):
        ppt = pp[ppi]
        pres = PPN[ppi]
        for j in range(8):
            bank = j // 4
            b.op("pe", lambda j=j, bank=bank, ppt=ppt: nc.tensor.matmul(
                ppt[:, bank * 512 + (j % 4) * P: bank * 512 + (j % 4 + 1) * P],
                stage_bf[:, j * P:(j + 1) * P], id_sb[:], start=True, stop=True),
                reads=R((src_res,), ("id",)), writes=R((pres, bank)), tag=f"{tg}.tr{q}{j}")
        b.op("act", lambda q=q, ppt=ppt: nc.scalar.activation(
            fmT[:, 0:8, q * P:(q + 1) * P], ppt[:].rearrange("p (j n) -> p j n", n=P), AF.Identity),
            reads=R((pres, 0), (pres, 1)), writes=R(("fmT", q)), tag=f"{tg}.trE{q}")

    def stage_ffn(sc, half):
        tg = f"F{half}.{sc}"
        for q in range(NQ):
            stats_ln(tg, q, True)
        for q in range(NQ):
            stats_fin(tg, q)
        batched_rstd(tg)
        for q in range(NQ):
            quant_scalars(tg, q)
        for q in range(NQ):
            b.op("dve", lambda q=q: nc.vector.tensor_scalar(xs[:], x_sb[:, q, :], col(q, "rs"),
                                                            col(q, "mbneg"), ALU.mult, ALU.add),
                 reads=R(("x", q), ("stc", q, 1)), writes=R(("xs",)), tag=f"{tg}.xs{q}")
            b.op("dve", lambda: nc.vector.tensor_copy(y1[:], xs[:]),
                 reads=R(("xs",)), writes=R(("y1",)), tag=f"{tg}.y1{q}")
            b.op("dve", lambda: nc.vector.scalar_tensor_tensor(ct[:], xs[:], -0.5, y1[:], ALU.add, ALU.is_gt),
                 reads=R(("xs",), ("y1",)), writes=R(("ct",)), tag=f"{tg}.cp{q}")
            b.op("dve", lambda: nc.vector.scalar_tensor_tensor(tt_f[:], xs[:], 0.5, y1[:], ALU.add, ALU.is_lt),
                 reads=R(("xs",), ("y1",)), writes=R(("ttf", 0), ("ttf", 1)), tag=f"{tg}.cn{q}")
            b.op("dve", lambda: nc.vector.tensor_tensor(ct[:], ct[:], tt_f[:], ALU.subtract),
                 reads=R(("ct",), ("ttf", 0), ("ttf", 1)), writes=R(("ct",)), tag=f"{tg}.cr{q}")
            b.op("dve", lambda: nc.vector.tensor_tensor(stage_bf[:], ct[:], y1[:], ALU.add),
                 reads=R(("ct",), ("y1",)), writes=R(("stage",)), tag=f"{tg}.q8{q}")
            transposes_to_fmT(tg, q, "stage", 2 + (q % 2))
        for q in range(NQ):
            ppi = q % 2
            ppt = pp[ppi]
            pres = PPN[ppi]
            for n in range(2):
                for j in range(8):
                    b.op("pe", lambda q=q, n=n, j=j, ppt=ppt: nc.tensor.matmul(
                        ppt[:, n * 512:(n + 1) * 512], fmT[:, j, q * P:(q + 1) * P],
                        tern[:, j, n * 512:(n + 1) * 512], start=(j == 0), stop=(j == 7)),
                        reads=R(("fmT", q), *[("tern", oc) for oc in range(4 * n, 4 * n + 4)]),
                        writes=R((pres, n)), tag=f"{tg}.mm{q}{n}{j}")
            for n in range(2):
                b.op("dve", lambda q=q, n=n, ppt=ppt: nc.vector.scalar_tensor_tensor(
                    tt_f[:, n * 512:(n + 1) * 512], ppt[:, n * 512:(n + 1) * 512], col(q, "alpha"),
                    bias_sb[:, n * 512:(n + 1) * 512], ALU.mult, ALU.add),
                    reads=R((pres, n), ("stc", q, 0), ("bias",)), writes=R(("ttf", n)), tag=f"{tg}.ep{q}{n}")
            b.op("pool", lambda q=q: nc.gpsimd.tensor_tensor(x_sb[:, q, :], x_sb[:, q, :], tt_f[:], ALU.add),
                 reads=R(("ttf", 0), ("ttf", 1), ("x", q)), writes=R(("x", q)), tag=f"{tg}.rs{q}")
            if half == 2:
                b.dma("act", f"so{q}", lambda sc=sc, q=q: nc.scalar.dma_start(
                    out=out[sc * SC + q * P: sc * SC + (q + 1) * P, :], in_=x_sb[:, q, :]),
                    reads=R(("x", q)), writes=R(("o", sc, q)), tag=f"xst{sc}.{q}")
                if sc + 1 < NSC:
                    emit_x_load(sc + 1, q)

    def stage_ln2(sc):
        tg = f"L.{sc}"
        for q in range(NQ):
            stats_ln(tg, q, False)
        for q in range(NQ):
            stats_fin(tg, q)
        batched_rstd(tg)
        for q in range(NQ):
            b.op("dve", lambda q=q: nc.vector.scalar_tensor_tensor(
                col(q, "mbneg"), col(q, "mean"), -1.0, col(q, "rstd"), ALU.mult, ALU.mult),
                reads=R(("stc", q, 2), ("stc", q, 1)), writes=R(("stc", q, 1)), tag=f"{tg}.mb{q}")
            b.op("dve", lambda q=q: nc.vector.tensor_scalar(stage_bf[:], x_sb[:, q, :], col(q, "rstd"),
                                                            col(q, "mbneg"), ALU.mult, ALU.add),
                 reads=R(("x", q), ("stc", q, 1)), writes=R(("stage",)), tag=f"{tg}.ap{q}")
            transposes_to_fmT(tg, q, "stage", 2 + (q % 2))

    def ktm_dst(h):
        return ktm[:].rearrange("p t (g n) -> p g t n", n=P)[:, 2 * h:2 * h + 2, :, :]

    def stage_qk(sc):
        tg = f"Q.{sc}"
        xs2 = xs[:].rearrange("p (a n) -> p a n", n=512)
        ct2 = ct[:].rearrange("p (a n) -> p a n", n=512)
        for h in range(H):
            if h + 1 < H:
                emit_rope_load(sc, h + 1)
            rts = rt_sb[h % 2] if mode == "linear" else rt_sb[0]
            rres = ("rt", h % 2) if mode == "linear" else ("rt", 0)
            for which, wname, dst, dres, tc, ts_ in ((0, "wq", qp, "qp", 0, 1),
                                                     (1, "wk", kp, "kp", 2, 3)):
                ppt = pp[which]
                pres = PPN[which]
                for cci in range(2):
                    dc = 2 * h + cci
                    for j in range(8):
                        s, g0, ncols = wslot(wname, j)
                        b.op("pe", lambda ppt=ppt, cci=cci, j=j, dc=dc, s=s, g0=g0, ncols=ncols: nc.tensor.matmul(
                            ppt[:, cci * 512:(cci + 1) * 512],
                            wg_sb[s][:].rearrange("p (c n) -> p c n", n=ncols)[:, j - g0, dc * P:(dc + 1) * P],
                            fmT[:, j, :], start=(j == 0), stop=(j == 7)),
                            reads=R(("wg", s), ("fmT", 0), ("fmT", 1), ("fmT", 2), ("fmT", 3)),
                            writes=R((pres, cci)), tag=f"{tg}.{wname}{dc}{j}")
                b.op("dve", lambda ppt=ppt, tc=tc, rts=rts: nc.vector.tensor_tensor(xs2[:, 0, :], ppt[:, 0:512], rts[:, tc, :], ALU.mult),
                     reads=R((pres, 0), rres), writes=R(("xs",)), tag=f"{tg}.ra{h}{which}")
                b.op("dve", lambda ppt=ppt, ts_=ts_, rts=rts: nc.vector.tensor_tensor(xs2[:, 1, :], ppt[:, 512:1024], rts[:, ts_, :], ALU.mult),
                     reads=R((pres, 1), rres), writes=R(("xs",)), tag=f"{tg}.rb{h}{which}")
                b.op("dve", lambda ppt=ppt, ts_=ts_, rts=rts: nc.vector.tensor_tensor(ct2[:, 0, :], ppt[:, 0:512], rts[:, ts_, :], ALU.mult),
                     reads=R((pres, 0), rres), writes=R(("ct",)), tag=f"{tg}.rc{h}{which}")
                b.op("dve", lambda ppt=ppt, tc=tc, rts=rts: nc.vector.tensor_tensor(ct2[:, 1, :], ppt[:, 512:1024], rts[:, tc, :], ALU.mult),
                     reads=R((pres, 1), rres), writes=R(("ct",)), tag=f"{tg}.rd{h}{which}")
                b.op("pool", lambda dst=dst, h=h: nc.gpsimd.tensor_tensor(dst[:, 2 * h, :], xs2[:, 0, :], xs2[:, 1, :], ALU.subtract),
                     reads=R(("xs",)), writes=R((dres, 2 * h)), tag=f"{tg}.r0{h}{which}")
                b.op("pool", lambda dst=dst, h=h: nc.gpsimd.tensor_tensor(dst[:, 2 * h + 1, :], ct2[:, 0, :], ct2[:, 1, :], ALU.add),
                     reads=R(("ct",)), writes=R((dres, 2 * h + 1)), tag=f"{tg}.r1{h}{which}")
            if mode == "linear":
                for cci in range(2):
                    for t in range(4):
                        b.op("pe", lambda cci=cci, t=t, h=h: nc.tensor.matmul(
                            pp[2][:, cci * 512 + t * P: cci * 512 + (t + 1) * P],
                            kp[:, 2 * h + cci, t * P:(t + 1) * P], id_sb[:], start=True, stop=True),
                            reads=R(("kp", 2 * h + cci), ("id",)), writes=R(("pp2", cci)), tag=f"{tg}.kt{h}{cci}{t}")
                b.op("act", lambda h=h: nc.scalar.activation(
                    ktm_dst(h), pp[2][:].rearrange("p (g t n) -> p g t n", t=4, n=P),
                    AF.Identity, scale=_gch()[h]),
                    reads=R(("pp2", 0), ("pp2", 1)), writes=R(("ktm", h)), tag=f"{tg}.ktE{h}")

    def store_kv_hist(sc):
        b.dma("act", "skst", lambda sc=sc: nc.scalar.dma_start(
            out=scr["k_hist"][sc], in_=kp[:]),
            reads=R(*[("kp", i) for i in range(8)]), writes=R(("d.kh", sc)), tag=f"kst{sc}")

    def store_v_hist(sc):
        b.dma("act", "svst", lambda sc=sc: nc.scalar.dma_start(
            out=scr["v_hist"][sc], in_=v_sb[:]),
            reads=R(*[("v", q, i) for q in range(NQ) for i in range(2)]),
            writes=R(("d.vh", sc)), tag=f"vst{sc}")

    def stage_retention_quad(sc):
        tg = f"Rq.{sc}"
        SCB = [("pp2", 0), ("pp2", 1), ("pp3", 0), ("pp3", 1)]
        SPP = [pp[2], pp[2], pp[3], pp[3]]
        SOF = [0, 512, 0, 512]
        for h in range(H):
            for bi, bb in enumerate(range(sc, -1, -1)):
                dlt = sc - bb
                ping = bi % 2
                if bb < sc:
                    b.dma("syncq", f"skh{ping}", lambda h=h, bb=bb, ping=ping: nc.sync.dma_start(
                        out=khb[:, ping, :, :], in_=scr["k_hist"][bb][:, 2 * h:2 * h + 2, :]),
                        reads=R(("d.kh", bb)), writes=R(("khb", ping)), tag=f"{tg}.lk{h}{bb}")
                    b.dma("act", f"svh{ping}", lambda h=h, bb=bb, ping=ping: nc.scalar.dma_start(
                        out=vhb[:, ping, :, :], in_=scr["v_hist"][bb][:, :, h * 512:(h + 1) * 512]),
                        reads=R(("d.vh", bb)), writes=R(("vhb", ping)), tag=f"{tg}.lv{h}{bb}")
                b.dma("act", f"smk{ping}", lambda h=h, dlt=dlt, ping=ping: nc.scalar.dma_start(
                    out=msk[:, ping, :, :], in_=A["c_qmask"][h, 4 * dlt:4 * dlt + 4].rearrange("m p n -> p m n")),
                    reads=R(), writes=R(("msk", ping)), tag=f"{tg}.lm{h}{dlt}")
                for jb in range(NQ):
                    spp, soff, sbank = SPP[jb], SOF[jb], SCB[jb]
                    for cci in range(2):
                        if bb == sc:
                            lhs = kp[:, 2 * h + cci, jb * P:(jb + 1) * P]
                            lres = ("kp", 2 * h + cci)
                        else:
                            lhs = khb[:, ping, cci, jb * P:(jb + 1) * P]
                            lres = ("khb", ping)
                        b.op("pe", lambda spp=spp, soff=soff, lhs=lhs, cci=cci, h=h: nc.tensor.matmul(
                            spp[:, soff:soff + 512], lhs, qp[:, 2 * h + cci, :],
                            start=(cci == 0), stop=(cci == 1)),
                            reads=R(lres, ("qp", 2 * h + cci)), writes=R(sbank), tag=f"{tg}.s{h}{bb}{jb}{cci}")
                    b.op("dve", lambda spp=spp, soff=soff, jb=jb, ping=ping: nc.vector.tensor_tensor(
                        sct[:, jb, :], spp[:, soff:soff + 512], msk[:, ping, 3 - jb, :], ALU.mult),
                        reads=R(sbank, ("msk", ping)), writes=R(("sct", jb)), tag=f"{tg}.e{h}{bb}{jb}")
                for jb in range(NQ):
                    for qb in range(NQ):
                        ppo = pp[qb // 2]
                        offo = (qb % 2) * 512
                        obank = (PPN[qb // 2], qb % 2)
                        if bb == sc:
                            vsrc = v_sb[:, jb, h * 512:(h + 1) * 512]
                            vres = ("v", jb, h // 2)
                        else:
                            vsrc = vhb[:, ping, jb, :]
                            vres = ("vhb", ping)
                        b.op("pe", lambda ppo=ppo, offo=offo, jb=jb, qb=qb, vsrc=vsrc, st=(bi == 0 and jb == 0), sp=(bi == sc and jb == NQ - 1): nc.tensor.matmul(
                            ppo[:, offo:offo + 512], sct[:, jb, qb * P:(qb + 1) * P], vsrc,
                            start=st, stop=sp),
                            reads=R(("sct", jb), vres), writes=R(obank), tag=f"{tg}.a{h}{bb}{jb}{qb}")
            for qb in range(NQ):
                ppt = pp[qb // 2]
                off = (qb % 2) * 512
                bank = (PPN[qb // 2], qb % 2)
                b.op("act", lambda ppt=ppt, off=off, h=h, qb=qb: nc.scalar.activation(
                    t2[:], ppt[:, off:off + 512], AF.Square, accum_out=acol(h, qb)),
                    reads=R(bank), writes=R(("sta", h, 0), ("t2",)), tag=f"{tg}.sq{h}{qb}")
            b.op("act", lambda h=h: nc.scalar.activation(arow(h, 80), arow(h, 64), AF.Sqrt,
                                                         bias=eps_sb[:], scale=1.0 / 512.0),
                 reads=R(("eps",), ("sta", h, 0)), writes=R(("sta", h, 1)), tag=f"{tg}.rq{h}")
            b.op("dve", lambda h=h: nc.vector.reciprocal(arow(h, 96), arow(h, 80)),
                 reads=R(("sta", h, 1)), writes=R(("sta", h, 2)), tag=f"{tg}.rr{h}")
            for qb in range(NQ):
                ppt = pp[qb // 2]
                off = (qb % 2) * 512
                bank = (PPN[qb // 2], qb % 2)
                b.op("dve", lambda ppt=ppt, off=off, h=h, qb=qb: nc.vector.tensor_scalar(
                    t1[:], ppt[:, off:off + 512], arc(h, qb), None, ALU.mult),
                    reads=R(bank, ("sta", h, 2)), writes=R(("t1",)), tag=f"{tg}.t1{h}{qb}")
                b.op("pool", lambda: nc.gpsimd.tensor_tensor(t2[:], t1[:], gnw_sb[:], ALU.mult),
                     reads=R(("t1",), ("gnw",)), writes=R(("t2",)), tag=f"{tg}.t2{h}{qb}")
                b.op("pool", lambda h=h, qb=qb: nc.gpsimd.tensor_tensor(
                    g_sb[:, qb, h * 512:(h + 1) * 512], t2[:], g_sb[:, qb, h * 512:(h + 1) * 512], ALU.mult),
                    reads=R(("t2",), ("g", qb, h // 2)), writes=R(("g", qb, h // 2)), tag=f"{tg}.og{h}{qb}")

    def stage_vg(sc, wname):
        tg = f"{wname}.{sc}"
        dst = v_sb if wname == "wv" else g_sb
        dres = "v" if wname == "wv" else "g"
        for q in range(NQ):
            for npair in range(2):
                ppt = pp[npair]
                pres = PPN[npair]
                for nh in range(2):
                    n = npair * 2 + nh
                    for j in range(8):
                        s, g0, ncols = wslot(wname, j)
                        b.op("pe", lambda ppt=ppt, nh=nh, n=n, j=j, q=q, s=s, g0=g0, ncols=ncols: nc.tensor.matmul(
                            ppt[:, nh * 512:(nh + 1) * 512], fmT[:, j, q * P:(q + 1) * P],
                            wg_sb[s][:].rearrange("p (c n) -> p c n", n=ncols)[:, j - g0, n * 512:(n + 1) * 512],
                            start=(j == 0), stop=(j == 7)),
                            reads=R(("wg", s), ("fmT", q)), writes=R((pres, nh)), tag=f"{tg}.m{q}{n}{j}")
                fn = AF.Identity if wname == "wv" else AF.Silu
                b.op("act", lambda ppt=ppt, q=q, npair=npair, dst=dst, fn=fn: nc.scalar.activation(
                    dst[:, q, npair * 1024:(npair + 1) * 1024], ppt[:], fn),
                    reads=R((pres, 0), (pres, 1)), writes=R((dres, q, npair)), tag=f"{tg}.e{q}{npair}")

    def stage_retention(sc):
        tg = f"R.{sc}"
        for h in range(H):
            for jb in range(NQ):
                nfree = SC - jb * P
                off = (jb % 2) * 512
                bank = ("pp3", jb % 2)
                for cci in range(2):
                    b.op("pe", lambda off=off, jb=jb, cci=cci, h=h, nfree=nfree: nc.tensor.matmul(
                        pp[3][:, off:off + nfree], kp[:, 2 * h + cci, jb * P:(jb + 1) * P],
                        qp[:, 2 * h + cci, jb * P:SC], start=(cci == 0), stop=(cci == 1)),
                        reads=R(("kp", 2 * h + cci), ("qp", 2 * h + cci)), writes=R(bank), tag=f"{tg}.s{h}{jb}{cci}")
                b.op("dve", lambda off=off, jb=jb, nfree=nfree: nc.vector.tensor_tensor(
                    sct[:, jb, 0:nfree], pp[3][:, off:off + nfree], tri_sb[:, 0:nfree], ALU.mult),
                    reads=R(bank, ("tri",)), writes=R(("sct", jb)), tag=f"{tg}.sE{h}{jb}")
            for qb in range(NQ):
                ppi = qb // 2
                ppt = pp[ppi]
                off = (qb % 2) * 512
                bank = (PPN[ppi], qb % 2)
                for jb in range(qb + 1):
                    b.op("pe", lambda ppt=ppt, off=off, jb=jb, qb=qb, h=h: nc.tensor.matmul(
                        ppt[:, off:off + 512], sct[:, jb, (qb - jb) * P:(qb - jb + 1) * P],
                        v_sb[:, jb, h * 512:(h + 1) * 512], start=(jb == 0), stop=False),
                        reads=R(("sct", jb), ("v", jb, h // 2)), writes=R(bank), tag=f"{tg}.i{h}{qb}{jb}")
                for cci in range(2):
                    b.op("pe", lambda ppt=ppt, off=off, cci=cci, qb=qb, h=h: nc.tensor.matmul(
                        ppt[:, off:off + 512], qp[:, 2 * h + cci, qb * P:(qb + 1) * P],
                        scr_s[:, 2 * h + cci, :], start=False, stop=(cci == 1)),
                        reads=R(("qp", 2 * h + cci), ("scr", h, cci)), writes=R(bank), tag=f"{tg}.x{h}{qb}{cci}")
                b.op("act", lambda ppt=ppt, off=off, h=h, qb=qb: nc.scalar.activation(
                    t2[:], ppt[:, off:off + 512], AF.Square, accum_out=acol(h, qb)),
                    reads=R(bank), writes=R(("sta", h, 0), ("t2",)), tag=f"{tg}.sq{h}{qb}")
            b.op("act", lambda h=h: nc.scalar.activation(arow(h, 80), arow(h, 64), AF.Sqrt,
                                                         bias=eps_sb[:], scale=1.0 / 512.0),
                 reads=R(("eps",), ("sta", h, 0)), writes=R(("sta", h, 1)), tag=f"{tg}.rq{h}")
            b.op("dve", lambda h=h: nc.vector.reciprocal(arow(h, 96), arow(h, 80)),
                 reads=R(("sta", h, 1)), writes=R(("sta", h, 2)), tag=f"{tg}.rr{h}")
            for qb in range(NQ):
                ppi = qb // 2
                ppt = pp[ppi]
                off = (qb % 2) * 512
                bank = (PPN[ppi], qb % 2)
                b.op("dve", lambda ppt=ppt, off=off, h=h, qb=qb: nc.vector.tensor_scalar(
                    t1[:], ppt[:, off:off + 512], arc(h, qb), None, ALU.mult),
                    reads=R(bank, ("sta", h, 2)), writes=R(("t1",)), tag=f"{tg}.t1{h}{qb}")
                b.op("pool", lambda: nc.gpsimd.tensor_tensor(t2[:], t1[:], gnw_sb[:], ALU.mult),
                     reads=R(("t1",), ("gnw",)), writes=R(("t2",)), tag=f"{tg}.t2{h}{qb}")
                b.op("pool", lambda h=h, qb=qb: nc.gpsimd.tensor_tensor(
                    g_sb[:, qb, h * 512:(h + 1) * 512], t2[:], g_sb[:, qb, h * 512:(h + 1) * 512], ALU.mult),
                    reads=R(("t2",), ("g", qb, h // 2)), writes=R(("g", qb, h // 2)), tag=f"{tg}.og{h}{qb}")
            for cci in range(2):
                b.op("pe", lambda cci=cci, h=h: nc.tensor.matmul(
                    pp[2][:, cci * 512:(cci + 1) * 512], id_sb[:], u_s[:, 2 * h + cci, :],
                    start=True, stop=False),
                    reads=R(("id",), ("u", h, cci)), writes=R(("pp2", cci)), tag=f"{tg}.up{h}{cci}")
            for t in range(4):
                for cci in range(2):
                    b.op("pe", lambda t=t, cci=cci, h=h: nc.tensor.matmul(
                        pp[2][:, cci * 512:(cci + 1) * 512], ktm[:, t, (2 * h + cci) * P:(2 * h + cci + 1) * P],
                        v_sb[:, t, h * 512:(h + 1) * 512], start=False, stop=(t == 3)),
                        reads=R(("ktm", h), ("v", t, h // 2)), writes=R(("pp2", cci)), tag=f"{tg}.u{h}{t}{cci}")
            for cci in range(2):
                b.op("dve", lambda cci=cci, h=h: nc.vector.tensor_copy(
                    scr_s[:, 2 * h + cci, :], pp[2][:, cci * 512:(cci + 1) * 512]),
                    reads=R(("pp2", cci)), writes=R(("scr", h, cci)), tag=f"{tg}.cE{h}{cci}")
                b.op("act", lambda cci=cci, h=h: nc.scalar.activation(
                    u_s[:, 2 * h + cci, :], pp[2][:, cci * 512:(cci + 1) * 512], AF.Identity, scale=_gch()[h]),
                    reads=R(("pp2", cci)), writes=R(("u", h, cci)), tag=f"{tg}.uE{h}{cci}")

    def stage_op(sc):
        tg = f"O.{sc}"
        for q in range(NQ):
            for d in range(16):
                ppt = pp[2] if d < 8 else pp[3]
                pres = PPN[2] if d < 8 else PPN[3]
                dd = d % 8
                bankh = dd // 4
                b.op("pe", lambda ppt=ppt, dd=dd, d=d, q=q, bankh=bankh: nc.tensor.matmul(
                    ppt[:, bankh * 512 + (dd % 4) * P: bankh * 512 + (dd % 4 + 1) * P],
                    g_sb[:, q, d * P:(d + 1) * P], id_sb[:], start=True, stop=True),
                    reads=R(("g", q, d // 8), ("id",)), writes=R((pres, bankh)), tag=f"{tg}.t{q}{d}")
            b.op("act", lambda: nc.scalar.activation(ogT[:, 0:8, :], pp[2][:].rearrange("p (j n) -> p j n", n=P), AF.Identity),
                 reads=R(("pp2", 0), ("pp2", 1)), writes=R(("ogT", 0)), tag=f"{tg}.tEa{q}")
            b.op("act", lambda: nc.scalar.activation(ogT[:, 8:16, :], pp[3][:].rearrange("p (j n) -> p j n", n=P), AF.Identity),
                 reads=R(("pp3", 0), ("pp3", 1)), writes=R(("ogT", 1)), tag=f"{tg}.tEb{q}")
            ppi = q % 2
            ppt = pp[ppi]
            pres = PPN[ppi]
            for n in range(2):
                for d in range(16):
                    s, g0, ncols = wslot("wo", d)
                    b.op("pe", lambda ppt=ppt, n=n, d=d, s=s, g0=g0, ncols=ncols: nc.tensor.matmul(
                        ppt[:, n * 512:(n + 1) * 512], ogT[:, d, :],
                        wg_sb[s][:].rearrange("p (c n) -> p c n", n=ncols)[:, d - g0, n * 512:(n + 1) * 512],
                        start=(d == 0), stop=(d == 15)),
                        reads=R(("ogT", d // 8), ("wg", s)), writes=R((pres, n)), tag=f"{tg}.m{q}{n}{d}")
            for n in range(2):
                b.op("dve", lambda ppt=ppt, q=q, n=n: nc.vector.tensor_tensor(
                    x_sb[:, q, n * 512:(n + 1) * 512], ppt[:, n * 512:(n + 1) * 512],
                    x_sb[:, q, n * 512:(n + 1) * 512], ALU.add),
                    reads=R((pres, n), ("x", q)), writes=R(("x", q)), tag=f"{tg}.r{q}{n}")

    bar_res = ([("vstage",), ("gstage",)]
               + [("v", q, npp) for q in range(NQ) for npp in range(2)]
               + [("g", q, npp) for q in range(NQ) for npp in range(2)]
               + [("gstage", k) for k in range(8)])
    b.op("pool", lambda: nc.gpsimd.memset(t1[:, 0:2], 0.0), writes=bar_res, tag="prepbarB")

    def dump_dram(nm, src_ap, reads, tag):
        if not dbg:
            return
        b.dma("pool", "sdbg", lambda nm=nm, src_ap=src_ap: nc.gpsimd.dma_start(out=DBG[nm], in_=src_ap),
              reads=reads, writes=R(("dbg", nm), ("dbgchain",)), tag="dumpd:" + tag)

    def dump(nm, src_ap, reads, tag, tok_major=False):
        if not dbg:
            return
        dst = DBG[nm].rearrange("(q p) d -> p q d", p=P) if tok_major else DBG[nm]
        b.dma("pool", "sdbg", lambda dst=dst, src_ap=src_ap: nc.gpsimd.dma_start(out=dst, in_=src_ap),
              reads=reads, writes=R(("dbg", nm), ("dbgchain",)), tag="dump:" + tag)

    for q in range(NQ):
        emit_x_load(0, q)
    for sc in range(NSC):
        if sc > 0:
            emit_granule_loads(sc, ("wq", "wk"))
            emit_rope_load(sc, 0)
        if sc == 0:
            dump_dram("d_wq", scr["wq_wt"], R(*[("d.wq.wt", i) for i in range(NWT["wq"])]), "wq")
            dump_dram("d_wv", scr["wv_wt"], R(*[("d.wv.wt", i) for i in range(NWT["wv"])]), "wv")
        stage_ffn(sc, 1)
        if sc == 0:
            dump("d_x1", x_sb[:], R(*[("x", q) for q in range(NQ)]), "x1", tok_major=True)
            dump("d_tern", tern[:], R(*[("tern", oc) for oc in range(8)]), "tern")
        stage_ln2(sc)
        if sc == 0:
            dump("d_fmT", fmT[:], R(*[("fmT", q) for q in range(NQ)]), "fmT")
        stage_qk(sc)
        if sc == 0:
            dump("d_qp", qp[:], R(*[("qp", i) for i in range(8)]), "qp")
            dump("d_kp", kp[:], R(*[("kp", i) for i in range(8)]), "kp")
            if mode == "linear":
                dump("d_ktm", ktm[:], R(*[("ktm", h) for h in range(H)]), "ktm")
        if mode == "quad":
            store_kv_hist(sc)
        emit_granule_loads(sc, ("wv",))
        stage_vg(sc, "wv")
        if mode == "quad":
            store_v_hist(sc)
        emit_granule_loads(sc, ("wg",))
        stage_vg(sc, "wg")
        if sc == 0:
            dump("d_v", v_sb[:], R(*[("v", q, i) for q in range(NQ) for i in range(2)]), "v")
            dump("d_g", g_sb[:], R(*[("g", q, i) for q in range(NQ) for i in range(2)]), "g")
        emit_granule_loads(sc, ("wo",))
        if mode == "linear":
            stage_retention(sc)
        else:
            stage_retention_quad(sc)
        if sc == 0:
            dump("d_og", g_sb[:], R(*[("g", q, i) for q in range(NQ) for i in range(2)]), "og")
            if mode == "linear":
                dump("d_s", scr_s[:], R(*[("scr", h, c) for h in range(H) for c in range(2)]), "s")
            dump("d_sct", sct[:], R(*[("sct", jb) for jb in range(NQ)]), "sct")
        stage_op(sc)
        if sc == 0:
            dump("d_x2", x_sb[:], R(*[("x", q) for q in range(NQ)]), "x2", tok_major=True)
        stage_ffn(sc, 2)
    b.op("syncq", lambda: nc.sync.nop(),
         reads=R(*[("o", sc, q) for sc in range(NSC) for q in range(NQ)]), tag="fin")

    b.finalize()

    with nc.Block() as block:
        @block.tensor
        def _(eng):
            b.emit(eng, "pe", sems)

        @block.vector
        def _(eng):
            b.emit(eng, "dve", sems)

        @block.scalar
        def _(eng):
            b.emit(eng, "act", sems)

        @block.gpsimd
        def _(eng):
            b.emit(eng, "pool", sems)

        @block.sync
        def _(eng):
            b.emit(eng, "syncq", sems)

    ctx.close()
    return nc


def kernel(**inputs):
    tabs = _host_tables()
    nc = build_program()
    x = np.ascontiguousarray(inputs["L0_attn_x"])
    base = {k: np.ascontiguousarray(v) for k, v in inputs.items() if k != "L0_attn_x"}
    base.update(tabs)
    in_maps = [dict(base, L0_attn_x=x[core]) for core in range(8)]
    res = run_bass_kernel_spmd(nc, in_maps, core_ids=list(range(8)))
    return np.stack([np.asarray(r["out"]) for r in res.results], axis=0)
